# version 19
# speedup vs baseline: 1.1072x; 1.0037x over previous
import math
from contextlib import ExitStack

import numpy as np
import ml_dtypes

import concourse.bass as bass
import concourse.mybir as mybir
from concourse.bass_utils import run_bass_kernel_spmd

F32 = mybir.dt.float32
BF16 = mybir.dt.bfloat16
I32 = mybir.dt.int32
AF = mybir.ActivationFunctionType
ALU = mybir.AluOpType
BF = ml_dtypes.bfloat16

B, T, D = 4, 4096, 2048
H, DH = 32, 64
INTER = 5504
NIT = 43
IPAD = 6144
S = 16
NT = 2048
NCT = 16
EPS = 1e-5

ENGINES = ("sync", "pe", "act", "dve", "pool")


class Buf:
    __slots__ = ("name", "w", "rs")

    def __init__(self, name):
        self.name = name
        self.w = {}
        self.rs = []


N_DSEM = 40

DRAM_BUFS = None


class Prog:

    def __init__(self, is_dram):
        self.ops = {e: [] for e in ENGINES}
        self.cum = {e: 0 for e in ENGINES}
        self.emit_pos = {e: 0 for e in ENGINES}
        self.know = {e: {} for e in ENGINES}
        self.is_dram = is_dram
        self.dcount = {}
        self.dkey2sem = {}
        self.dsem_free = [f"d{i}" for i in range(N_DSEM)]
        self.dsem_inuse = []
        self.nbar = 0

    def _dsem_for(self, key):
        if key not in self.dkey2sem:
            if not self.dsem_free:
                raise RuntimeError("out of dma sems; add a barrier")
            nm = self.dsem_free.pop(0)
            self.dsem_inuse.append(nm)
            self.dkey2sem[key] = nm
        return self.dkey2sem[key]

    def _semkey(self, reads, writes):
        for b in writes:
            if not self.is_dram(b.name):
                return b.name
        for b in reads:
            if not self.is_dram(b.name):
                return b.name
        return "misc"

    def add(self, eng, fn, reads=(), writes=(), inc=None, deps=()):
        is_dma = inc is not None and inc == 16 and eng in ("sync", "pool")
        is_cc = inc == 1 and eng == "pool"
        if inc is None:
            inc = 16 if eng == "sync" else 1
            is_dma = eng == "sync"
        dd = {}

        def _merge(d):
            if d is not None:
                sn, v = d
                dd[sn] = max(dd.get(sn, 0), v)

        for b in reads:
            for it in b.w.items():
                _merge(it)
            if "_ps" in b.name:
                for r in b.rs:
                    if r[0] != eng:
                        _merge(r)
        for b in writes:
            for it in b.w.items():
                _merge(it)
            for r in b.rs:
                _merge(r)
        for d in deps:
            _merge(d)
        if self.nbar:
            dd["bar"] = max(dd.get("bar", 0), self.nbar)

        if is_dma or is_cc:
            key = self._semkey(reads, writes)
            sn = self._dsem_for(key)
            self.dcount[sn] = self.dcount.get(sn, 0) + inc
            tok = (sn, self.dcount[sn])
            self.ops[eng].append((fn, (sn, inc), None, dd))
        else:
            self.cum[eng] += inc
            tok = (eng, self.cum[eng])
            self.ops[eng].append((fn, (eng, inc), None, dd))
        for b in reads:
            if is_dma or is_cc:
                b.rs.append((tok[0], self.dcount[tok[0]]))
            else:
                b.rs.append(tok)
        for b in writes:
            if is_dma or is_cc:
                b.w[tok[0]] = self.dcount[tok[0]]
            else:
                b.w = {tok[0]: tok[1]}
            b.rs = []
        return tok

    def barrier(self):
        self.nbar += 1
        for qeng in ("sync", "pool"):
            dd = {sn: self.dcount[sn] for sn in self.dsem_inuse
                  if self.dcount.get(sn, 0)}
            self.ops[qeng].append(
                (lambda e: e.nop(), ("bar", 1 if qeng == "sync" else 0), None, dd))
        self.bar_cums = dict(self.cum)
        self._bar_extra = {e: self.cum[e] for e in ("pe", "act", "dve", "pool")
                           if self.cum[e]}
        self.ops["sync"][-1][3].update(
            {e: v for e, v in self._bar_extra.items()})
        self.ops["pool"][-1][3].update(
            {e: v for e, v in self._bar_extra.items()
             if e != "pool"})
        self.dkey2sem = {}
        self.dsem_free = [f"d{i}" for i in range(N_DSEM)]
        self.dsem_inuse = []

    def emit_phase(self, ename, eng, sems):
        know = self.know[ename]
        ops = self.ops[ename]
        pos = self.emit_pos[ename]
        for fn, (sn, inc), _, dd in ops[pos:]:
            for dn, v in sorted(dd.items()):
                if dn == "pe" and ename == "pe":
                    continue
                if v > know.get(dn, 0):
                    eng.wait_ge(sems[dn], v)
                    know[dn] = v
            inst = fn(eng)
            if inst is not None and inc:
                inst.then_inc(sems[sn], inc)
        self.emit_pos[ename] = len(ops)


def _emit_block(nc, prog, sems, final=False):
    with nc.Block() as blk:
        blk.sync(lambda e: (prog.emit_phase("sync", e, sems),
                            final and _final_waits(e, prog, sems)))
        blk.tensor(lambda e: prog.emit_phase("pe", e, sems))
        blk.scalar(lambda e: prog.emit_phase("act", e, sems))
        blk.vector(lambda e: prog.emit_phase("dve", e, sems))
        blk.gpsimd(lambda e: prog.emit_phase("pool", e, sems))


def _final_waits(eng, prog, sems):
    for e2 in ENGINES:
        if prog.cum[e2]:
            eng.wait_ge(sems[e2], prog.cum[e2])
    for sn, v in prog.dcount.items():
        if v:
            eng.wait_ge(sems[sn], v)
    return True


def build(taps=frozenset(), upto=99, p7sub=4):
    nc = bass.Bass("TRN2", target_bir_lowering=False)
    taps = set(taps)

    def din(name, shape, dtype=F32):
        return nc.dram_tensor(name, shape, dtype, kind="ExternalInput").ap()

    tapreqs = []

    def dint(name, shape, dtype):
        if name in taps:
            iap = nc.dram_tensor(name + "_i", shape, dtype, kind="Internal").ap()
            oap = nc.dram_tensor(name, shape, dtype, kind="ExternalOutput").ap()
            tapreqs.append((name, iap, oap, shape, dtype))
            return iap
        return nc.dram_tensor(name, shape, dtype, kind="Internal").ap()

    x0 = din("x0", [D, NT])
    wqT = din("wqT", [D, 1024])
    wkT = din("wkT", [D, 1024])
    wvT = din("wvT", [D, 1024])
    woT = din("woT", [D, D])
    bq = din("bq", [128, 8])
    bk = din("bk", [128, 8])
    bv = din("bv", [1, 1024])
    bo = din("bo", [128, 16])
    pinT = din("pinT", [D, D])
    pinb = din("pinb", [128, 16])
    poutT = din("poutT", [D, D])
    poutb = din("poutb", [128, 16])
    BwT = din("BwT", [D, S])
    CwT = din("CwT", [S, D])
    Amat = din("Amat", [D, 8])
    Dpar = din("Dpar", [128, 16])
    lut = din("lut", [128, 16])
    gidxT = din("gidxT", [D, 768], I32)
    uidxT = din("uidxT", [D, 768], I32)
    didxT = din("didxT", [INTER, 256], I32)
    ropeCq = din("ropeCq", [128, T])
    ropeSq = din("ropeSq", [128, T])
    ropeCk = din("ropeCk", [128, T])
    ropeSk = din("ropeSk", [128, T])
    dmask = din("dmask", [128, 2048], BF16)
    Eall = din("Eall", [128, 64], BF16)
    cvals = din("cvals", [128, 8])
    identb = din("identb", [128, 128], BF16)
    sel0 = din("sel0", [128, 1])
    sel1 = din("sel1", [128, 1])

    GN = 16 * 6 * 128 * 128
    DN = 43 * 2 * 128 * 128
    WSH_N = 2 * GN + DN
    xn1_my = dint("xn1_my", [D, NT], BF16)
    xn1_g = dint("xn1_g", [2 * D, NT], BF16)
    qrot = dint("qrot", [1024, T], BF16)
    krot = dint("krot", [1024, T], BF16)
    vt = dint("vt", [1024, T], BF16)
    attn_my = dint("attn_my", [1024, T], BF16)
    attn_g = dint("attn_g", [2048, T], BF16)
    wsh = dint("wsh", [1, WSH_N], BF16)
    wg8 = dint("wg8", [8, WSH_N], BF16)
    x1 = dint("x1", [D, NT], F32)
    xn2 = dint("xn2", [D, NT], BF16)
    x2 = dint("x2", [D, NT], F32)
    xn3 = dint("xn3", [D, NT], BF16)
    x3 = dint("x3", [D, NT], F32)
    xn4 = dint("xn4", [D, NT], BF16)
    xp_my = dint("xp_my", [D, NT], BF16)
    xp_g = dint("xp_g", [2 * D, NT], BF16)
    dT_my = dint("dT_my", [S, NT], BF16)
    dT_g = dint("dT_g", [2 * S, NT], BF16)
    hbuf = dint("hbuf", [INTER, NT], BF16)
    wo_bf = dint("wo_bf", [D, D], BF16)
    pin_bf = dint("pin_bf", [D, D], BF16)
    pout_bf = dint("pout_bf", [D, D], BF16)
    G_my = dint("G_my", [8, T], BF16)
    G_g = dint("G_g", [16, T], BF16)
    xout = nc.dram_tensor("xout", [D, NT], F32, kind="ExternalOutput").ap()
    import os as _osd
    DBG7 = _osd.environ.get("DBG7", "0") == "1"
    if DBG7:
        dbg_pout = nc.dram_tensor("dbg_pout", [128, 32768], BF16,
                                  kind="ExternalOutput").ap()
        dbg_y = nc.dram_tensor("dbg_y", [128, 32768], BF16,
                               kind="ExternalOutput").ap()

    _DRAM_NAMES = {"ln1_dst", "xn1_g", "qkstage", "vtstage", "attn_my",
                   "attn_g", "wsh", "wg8", "x1", "x2", "x3", "xout", "hbuf",
                   "xp_my", "xp_g", "dT_my", "dT_g", "G_my", "G_g",
                   "xn3_dst", "stub", "dbgp", "dbgy", "misc", "wo_bf", "pin_bf", "pout_bf", "wobf", "pinbf", "poutbf"}

    def _is_dram(name):
        return (name in _DRAM_NAMES or name.endswith("_tapout")
                or name.endswith("_dst") or name.endswith("_xo")
                or name.endswith("_xnsrc") or name.endswith("_xressrc"))

    prog = Prog(_is_dram)
    bufs = {}

    def B_(name):
        if name not in bufs:
            bufs[name] = Buf(name)
        return bufs[name]

    with ExitStack() as outer:
        sems = {e: outer.enter_context(nc.semaphore(f"sem_{e}")) for e in ENGINES}
        sems["bar"] = outer.enter_context(nc.semaphore("sem_bar"))
        for i in range(N_DSEM):
            sems[f"d{i}"] = outer.enter_context(nc.semaphore(f"dsem_{i}"))

        sb = lambda n, s, d: outer.enter_context(nc.sbuf_tensor(n, s, d))
        cv = sb("cv", [128, 8], F32)
        lut_s = sb("lut_s", [128, 16], F32)
        E_s = sb("E_s", [128, 64], BF16)
        ones1 = sb("ones1", [1, 128], F32)
        ones1b = sb("ones1b", [1, 128], BF16)
        onesC = sb("onesC", [128, 1], F32)
        onesCb = sb("onesCb", [128, 8], BF16)
        sigA = sb("sigA", [128, 128], F32)
        sigAb = sb("sigAb", [128, 128], BF16)
        BwT_s = sb("BwT_s", [128, 256], BF16)
        CwT_s = sb("CwT_s", [16, D], BF16)
        D_s = sb("D_s", [128, 16], F32)
        bq_s = sb("bq_s", [128, 8], F32)
        bk_s = sb("bk_s", [128, 8], F32)
        bo_s = sb("bo_s", [128, 16], F32)
        pinb_s = sb("pinb_s", [128, 16], F32)
        poutb_s = sb("poutb_s", [128, 16], F32)
        bvb = sb("bvb", [128, 1024], F32)
        mask_s = sb("mask_s", [128, 2048], BF16)
        identb_s = sb("identb_s", [128, 128], BF16)
        bidx = {en: sb(f"p4_bidx_{en}", [128, 768], I32) for en in ("dve",)}
        bidf = {en: sb(f"p4_bidf_{en}", [128, 768], BF16) for en in ("dve",)}
        bcon = {en: sb(f"p4_bcon_{en}", [128, 768], BF16) for en in ("dve",)}
        bacc = {en: sb(f"p4_bacc_{en}", [128, 768], BF16) for en in ("dve",)}
        sel0c = sb("sel0c", [128, 1], F32)
        sel1c = sb("sel1c", [128, 1], F32)

        with ExitStack() as ph:
            psb = lambda n, s, d: ph.enter_context(nc.sbuf_tensor(n, s, d))
            pps = lambda n, s: ph.enter_context(nc.psum_tensor(n, s, F32))
            tmpA = psb("p0_tmpA", [128, 256], F32)
            tmpB = psb("p0_tmpB", [16, D], F32)
            bvrow = psb("p0_bvrow", [1, 1024], F32)
            psb0 = pps("p0_ps", [128, 512])

            for nm, dst, src in [("cv", cv, cvals), ("lut_s", lut_s, lut),
                                 ("D_s", D_s, Dpar), ("bq_s", bq_s, bq),
                                 ("bk_s", bk_s, bk), ("bo_s", bo_s, bo),
                                 ("pinb_s", pinb_s, pinb),
                                 ("poutb_s", poutb_s, poutb)]:
                prog.add("sync", lambda e, d=dst, s=src: e.dma_start(out=d[:], in_=s),
                         writes=[B_("c_" + nm)])
            prog.add("sync", lambda e: e.dma_start(out=E_s[:], in_=Eall),
                     writes=[B_("E_s")])
            prog.add("sync", lambda e: e.dma_start(out=identb_s[:], in_=identb),
                     writes=[B_("c_identb")])
            prog.add("sync", lambda e: e.dma_start(out=sel0c[:], in_=sel0),
                     writes=[B_("c_sel")])
            prog.add("sync", lambda e: e.dma_start(out=sel1c[:], in_=sel1),
                     writes=[B_("c_sel")])
            prog.add("sync", lambda e: e.dma_start(out=mask_s[:], in_=dmask),
                     writes=[B_("mask_s")])
            prog.add("sync", lambda e: e.dma_start(out=bvrow[:], in_=bv),
                     writes=[B_("bvrow")])
            prog.add("dve", lambda e: e.memset(ones1[:], 1.0), writes=[B_("ones1")])
            prog.add("dve", lambda e: e.memset(ones1b[:], 1.0), writes=[B_("ones1b")])
            prog.add("dve", lambda e: e.memset(onesC[:], 1.0), writes=[B_("onesC")])
            prog.add("dve", lambda e: e.memset(onesCb[:], 1.0), writes=[B_("onesCb")])
            prog.add("sync", lambda e: e.dma_start(
                out=tmpA[:, 0:128].rearrange("p (ct s) -> p ct s", ct=16),
                in_=Amat.rearrange("(ct p) s -> p ct s", p=128)),
                writes=[B_("tmpA")])
            prog.add("act", lambda e: e.activation(sigA[:], tmpA[:, 0:128],
                                                   AF.Sigmoid),
                     reads=[B_("tmpA")], writes=[B_("sigA")])
            prog.add("dve", lambda e: e.tensor_copy(sigAb[:], sigA[:]),
                     reads=[B_("sigA")], writes=[B_("sigAb")])
            prog.add("sync", lambda e: e.dma_start(
                out=tmpA[:].rearrange("p (ct s) -> p ct s", ct=16),
                in_=BwT.rearrange("(ct p) s -> p ct s", p=128)),
                writes=[B_("tmpA")])
            prog.add("dve", lambda e: e.tensor_copy(BwT_s[:], tmpA[:]),
                     reads=[B_("tmpA")], writes=[B_("BwT_s")])
            prog.add("sync", lambda e: e.dma_start(out=tmpB[:], in_=CwT),
                     writes=[B_("tmpB")])
            prog.add("dve", lambda e: e.tensor_copy(CwT_s[:], tmpB[:]),
                     reads=[B_("tmpB")], writes=[B_("CwT_s")])
            for h in range(2):
                prog.add("pe", lambda e, h=h: e.matmul(
                    psb0[:, 0:512], ones1[:], bvrow[0:1, h * 512:(h + 1) * 512],
                    start=True, stop=True),
                    reads=[B_("ones1"), B_("bvrow")], writes=[B_("p0_ps")])
                prog.add("dve", lambda e, h=h: e.tensor_copy(
                    bvb[:, h * 512:(h + 1) * 512], psb0[:, 0:512]),
                    reads=[B_("p0_ps")], writes=[B_("bvb")])
            prog.barrier()
            _emit_block(nc, prog, sems)

        def ln_pass(tag, src_ap, dst_ap, extra_dst_my=None):
            with ExitStack() as ph:
                psb = lambda n, s, d: ph.enter_context(nc.sbuf_tensor(n, s, d))
                pps = lambda n, s: ph.enter_context(nc.psum_tensor(n, s, F32))
                xt = [[psb(f"{tag}_x{i}_{pr}", [128, 512], F32)
                       for i in range(NCT)] for pr in (0, 1)]
                sq = [psb(f"{tag}_sq{pr}", [128, 512], F32) for pr in (0, 1)]
                rowm = [psb(f"{tag}_m{pr}", [1, 512], F32) for pr in (0, 1)]
                rowq = [psb(f"{tag}_q{pr}", [1, 512], F32) for pr in (0, 1)]
                rowr = [psb(f"{tag}_r{pr}", [1, 512], F32) for pr in (0, 1)]
                rowmb = [psb(f"{tag}_mb{pr}", [1, 512], F32) for pr in (0, 1)]
                xnt = [psb(f"{tag}_xn{pr}", [128, 512], BF16) for pr in (0, 1)]
                t1 = [psb(f"{tag}_t1{pr}", [128, 512], F32) for pr in (0, 1)]
                psA = [pps(f"{tag}_psA{pr}", [1, 512]) for pr in (0, 1)]
                psB = [pps(f"{tag}_psB{pr}", [1, 512]) for pr in (0, 1)]
                psR = [pps(f"{tag}_psR{pr}", [128, 512]) for pr in (0, 1)]
                psM = [pps(f"{tag}_psM{pr}", [128, 512]) for pr in (0, 1)]
                for tc in range(4):
                    pr = tc % 2
                    bx = [B_(f"{tag}_x{i}_{pr}") for i in range(NCT)]
                    for ct in range(NCT):
                        prog.add("sync", lambda e, ct=ct, tc=tc, pr=pr: e.dma_start(
                            out=xt[pr][ct][:],
                            in_=src_ap[ct * 128:(ct + 1) * 128,
                                       tc * 512:(tc + 1) * 512]),
                            writes=[bx[ct]])
                        prog.add("pe", lambda e, ct=ct, pr=pr: e.matmul(
                            psA[pr][:], onesC[:], xt[pr][ct][:], start=(ct == 0),
                            stop=(ct == NCT - 1)),
                            reads=[bx[ct], B_("onesC")], writes=[B_(f"{tag}_psA{pr}")])
                        prog.add("act", lambda e, ct=ct, pr=pr: e.activation(
                            sq[pr][:], xt[pr][ct][:], AF.Square),
                            reads=[bx[ct]], writes=[B_(f"{tag}_sq{pr}")])
                        prog.add("pe", lambda e, ct=ct, pr=pr: e.matmul(
                            psB[pr][:], onesC[:], sq[pr][:], start=(ct == 0),
                            stop=(ct == NCT - 1)),
                            reads=[B_(f"{tag}_sq{pr}"), B_("onesC")],
                            writes=[B_(f"{tag}_psB{pr}")])
                    prog.add("dve", lambda e, pr=pr: e.tensor_scalar_mul(
                        rowm[pr][:], psA[pr][:], 1.0 / D),
                        reads=[B_(f"{tag}_psA{pr}")], writes=[B_(f"{tag}_m{pr}")])
                    prog.add("dve", lambda e, pr=pr: e.tensor_scalar_mul(
                        rowq[pr][:], psB[pr][:], 1.0 / D),
                        reads=[B_(f"{tag}_psB{pr}")], writes=[B_(f"{tag}_q{pr}")])
                    prog.add("dve", lambda e, pr=pr: e.scalar_tensor_tensor(
                        rowr[pr][:], rowm[pr][:], 1.0, rowm[pr][:],
                        ALU.mult, ALU.mult),
                        reads=[B_(f"{tag}_m{pr}")], writes=[B_(f"{tag}_r{pr}")])
                    prog.add("dve", lambda e, pr=pr: e.tensor_sub(
                        rowq[pr][:], rowq[pr][:], rowr[pr][:]),
                        reads=[B_(f"{tag}_q{pr}"), B_(f"{tag}_r{pr}")],
                        writes=[B_(f"{tag}_q{pr}")])
                    prog.add("act", lambda e, pr=pr: e.activation(
                        rowr[pr][:], rowq[pr][:], AF.Ln, bias=cv[0:1, 0:1]),
                        reads=[B_(f"{tag}_q{pr}"), B_("c_cv")],
                        writes=[B_(f"{tag}_r{pr}")])
                    prog.add("act", lambda e, pr=pr: e.activation(
                        rowr[pr][:], rowr[pr][:], AF.Exp, scale=cv[0:1, 1:2]),
                        reads=[B_(f"{tag}_r{pr}")], writes=[B_(f"{tag}_r{pr}")])
                    prog.add("dve", lambda e, pr=pr: e.scalar_tensor_tensor(
                        rowmb[pr][:], rowm[pr][:], -1.0, rowr[pr][:],
                        ALU.mult, ALU.mult),
                        reads=[B_(f"{tag}_m{pr}"), B_(f"{tag}_r{pr}")],
                        writes=[B_(f"{tag}_mb{pr}")])
                    prog.add("pe", lambda e, pr=pr: e.matmul(
                        psR[pr][:], ones1[:], rowr[pr][:], start=True, stop=True),
                        reads=[B_(f"{tag}_r{pr}"), B_("ones1")],
                        writes=[B_(f"{tag}_psR{pr}")])
                    prog.add("pe", lambda e, pr=pr: e.matmul(
                        psM[pr][:], ones1[:], rowmb[pr][:], start=True, stop=True),
                        reads=[B_(f"{tag}_mb{pr}"), B_("ones1")],
                        writes=[B_(f"{tag}_psM{pr}")])
                    for ct in range(NCT):
                        prog.add("dve", lambda e, ct=ct, pr=pr: e.tensor_mul(
                            t1[pr][:], xt[pr][ct][:], psR[pr][:]),
                            reads=[bx[ct], B_(f"{tag}_psR{pr}")],
                            writes=[B_(f"{tag}_t1{pr}")])
                        prog.add("dve", lambda e, pr=pr: e.tensor_add(
                            xnt[pr][:], t1[pr][:], psM[pr][:]),
                            reads=[B_(f"{tag}_t1{pr}"), B_(f"{tag}_psM{pr}")],
                            writes=[B_(f"{tag}_xn{pr}")])
                        prog.add("sync", lambda e, ct=ct, tc=tc, pr=pr: e.dma_start(
                            out=dst_ap[ct * 128:(ct + 1) * 128,
                                       tc * 512:(tc + 1) * 512],
                            in_=xnt[pr][:]),
                            reads=[B_(f"{tag}_xn{pr}")], writes=[B_(tag + "_dst")])
                prog.barrier()
                _emit_block(nc, prog, sems)

        if upto >= 1:
            ln_pass("ln1", x0, xn1_my)

        def emit_taps():
            for name, iap, oap, shape, dtype in tapreqs:
                prog.add("pool", lambda e, i=iap, o=oap: e.dma_start(out=o, in_=i),
                         reads=[B_(name.replace("_i", ""))], deps=(),
                         writes=[B_(name + "_tapout")], inc=16)

        G_OFF = 0
        U_OFF = D * 768
        D_OFF = 2 * D * 768
        btiles = []
        for ct in range(NCT):
            btiles.append((gidxT, ct * 128, 768, G_OFF + ct * 128 * 768))
        for ct in range(NCT):
            btiles.append((uidxT, ct * 128, 768, U_OFF + ct * 128 * 768))
        for it in range(NIT):
            btiles.append((didxT, it * 128, 256, D_OFF + it * 128 * 256))
        build_thunks = []

        def emit_btile(src, r0, w, off):
            en2 = "dve"
            build_thunks.append(lambda src=src, r0=r0, w=w, en2=en2: prog.add(
                "sync", lambda e: e.dma_start(
                    out=bidx[en2][:, 0:w], in_=src[r0:r0 + 128, :]),
                writes=[B_(f"p4_bidx_{en2}")], inc=16))
            build_thunks.append(lambda w=w, en2=en2: prog.add(
                en2, lambda e: e.tensor_copy(
                    bidf[en2][:, 0:w], bidx[en2][:, 0:w]),
                reads=[B_(f"p4_bidx_{en2}")],
                writes=[B_(f"p4_bidf_{en2}")]))
            for j in range(16):
                if j == 0:
                    build_thunks.append(lambda w=w, en2=en2: prog.add(
                        en2, lambda e: e.tensor_scalar(
                            bacc[en2][:, 0:w], bidf[en2][:, 0:w], 0.0,
                            lut_s[:, 0:1], ALU.is_equal, ALU.mult),
                        reads=[B_(f"p4_bidf_{en2}"), B_("c_lut_s")],
                        writes=[B_(f"p4_bacc_{en2}")]))
                else:
                    build_thunks.append(lambda w=w, j=j, en2=en2: prog.add(
                        en2, lambda e: e.tensor_scalar(
                            bcon[en2][:, 0:w], bidf[en2][:, 0:w], float(j),
                            lut_s[:, j:j + 1], ALU.is_equal, ALU.mult),
                        reads=[B_(f"p4_bidf_{en2}"), B_("c_lut_s")],
                        writes=[B_(f"p4_bcon_{en2}")]))
                    build_thunks.append(lambda w=w, en2=en2: prog.add(
                        en2, lambda e: e.tensor_add(
                            bacc[en2][:, 0:w], bacc[en2][:, 0:w],
                            bcon[en2][:, 0:w]),
                        reads=[B_(f"p4_bcon_{en2}")],
                        writes=[B_(f"p4_bacc_{en2}")]))
            nt_r = 16 if w == 768 else NIT
            region = G_OFF if off < U_OFF else (U_OFF if off < D_OFF else D_OFF)
            ti = (off - region) // (128 * w)
            build_thunks.append(lambda region=region, ti=ti, w=w,
                                nt_r=nt_r, en2=en2: prog.add(
                "sync", lambda e: e.dma_start(
                    out=wsh[0, region:region + 128 * nt_r * w]
                    .rearrange("(p t i) -> p t i", p=128, t=nt_r)[:, ti, :],
                    in_=bacc[en2][:, 0:w]),
                reads=[B_(f"p4_bacc_{en2}")], writes=[B_("wsh")], inc=16))

        for bt in btiles:
            emit_btile(*bt)
        bt_i = [0]

        def drain_build(k):
            while k > 0 and bt_i[0] < len(build_thunks):
                build_thunks[bt_i[0]]()
                bt_i[0] += 1
                k -= 1

        if upto >= 2:
            with ExitStack() as ph:
                psb = lambda n, s, d: ph.enter_context(nc.sbuf_tensor(n, s, d))
                pps = lambda n, s: ph.enter_context(nc.psum_tensor(n, s, F32))
                wq_s = psb("p2_wq", [128, 16 * 1024], BF16)
                wk_s = psb("p2_wk", [128, 16 * 1024], BF16)
                wv_s = psb("p2_wv", [128, 16 * 1024], BF16)
                wtmp = psb("p2_wtmp", [128, 1024], F32)
                cvt32 = [psb(f"p2_cvt32{i}", [128, 2048], F32) for i in (0, 1)]
                cvt16 = [psb(f"p2_cvt16{i}", [128, 2048], BF16) for i in (0, 1)]
                prog.add("pool", lambda e: e.collective_compute(
                    "AllGather", ALU.bypass, [[0, 1], [2, 3], [4, 5], [6, 7]],
                    ins=[xn1_my.opt()], outs=[xn1_g.opt()]),
                    reads=[B_("ln1_dst")], writes=[B_("xn1_g")], inc=1)
                xch = [psb(f"p2_xch{i}", [128, 16 * 512], BF16) for i in (0, 1)]
                Ctq = psb("p2_Ctq", [128, 512], F32)
                Stq = psb("p2_Stq", [128, 512], F32)
                Ctk = psb("p2_Ctk", [128, 512], F32)
                Stk = psb("p2_Stk", [128, 512], F32)
                qf = [psb(f"p2_qf{i}", [128, 512], F32) for i in (0, 1)]
                qsw = [psb(f"p2_qsw{i}", [128, 512], F32) for i in (0, 1)]
                qt1 = [psb(f"p2_qt1{i}", [128, 512], F32) for i in (0, 1)]
                qt2 = [psb(f"p2_qt2{i}", [128, 512], F32) for i in (0, 1)]
                qbf = [psb(f"p2_qbf{i}", [128, 512], BF16) for i in (0, 1)]
                vbf = psb("p2_vbf", [128, 512], BF16)
                psq = pps("p2_psq", [128, 512])
                psk = pps("p2_psk", [128, 512])
                psv = [pps(f"p2_psv{i}", [128, 512]) for i in (0, 1)]
                for wsrc, wdst_d, nm in ((woT, wo_bf, "wobf"),
                                         (pinT, pin_bf, "pinbf"),
                                         (poutT, pout_bf, "poutbf")):
                    for ct in range(NCT):
                        prog.add("sync", lambda e, srcw=wsrc, ct=ct: e.dma_start(
                            out=cvt32[ct % 2][:],
                            in_=srcw[ct * 128:(ct + 1) * 128, :]),
                            writes=[B_(f"p2_cvt32{ct % 2}")])
                        prog.add("dve", lambda e, ct=ct: e.tensor_copy(
                            cvt16[ct % 2][:], cvt32[ct % 2][:]),
                            reads=[B_(f"p2_cvt32{ct % 2}")],
                            writes=[B_(f"p2_cvt16{ct % 2}")])
                        prog.add("sync", lambda e, d=wdst_d, ct=ct: e.dma_start(
                            out=d[ct * 128:(ct + 1) * 128, :],
                            in_=cvt16[ct % 2][:]),
                            reads=[B_(f"p2_cvt16{ct % 2}")], writes=[B_(nm)])
                for wsrc, wdst, nm in ((wqT, wq_s, "wq"), (wkT, wk_s, "wk"),
                                       (wvT, wv_s, "wv")):
                    for ct in range(NCT):
                        prog.add("sync", lambda e, s=wsrc, ct=ct: e.dma_start(
                            out=wtmp[:], in_=s[ct * 128:(ct + 1) * 128, :]),
                            writes=[B_("p2_wtmp")])
                        prog.add("dve", lambda e, d=wdst, ct=ct: e.tensor_copy(
                            d[:, ct * 1024:(ct + 1) * 1024], wtmp[:]),
                            reads=[B_("p2_wtmp")], writes=[B_("p2_" + nm)])
                for tc in range(8):
                    drain_build(160)
                    slot, tcl = tc // 4, tc % 4
                    xb = tc % 2
                    for ct in range(NCT):
                        prog.add("sync", lambda e, ct=ct, slot=slot, tcl=tcl,
                                 xb=xb: e.dma_start(
                                     out=xch[xb][:, ct * 512:(ct + 1) * 512],
                                     in_=xn1_g[slot * D + ct * 128:
                                               slot * D + (ct + 1) * 128,
                                               tcl * 512:(tcl + 1) * 512]),
                                 reads=[B_("xn1_g")], writes=[B_(f"p2_xch{xb}")])
                    for tab, src in ((Ctq, ropeCq), (Stq, ropeSq),
                                     (Ctk, ropeCk), (Stk, ropeSk)):
                        prog.add("sync", lambda e, t=tab, s=src, tc=tc: e.dma_start(
                            out=t[:], in_=s[:, tc * 512:(tc + 1) * 512]),
                            writes=[B_("p2_tab")])
                    for ot in range(8):
                        for si, (wbuf, wnm, bcol, Ct, St, dst, psx, psnm) in \
                                enumerate((
                                (wq_s, "wq", bq_s, Ctq, Stq, qrot, psq, "p2_psq"),
                                (wk_s, "wk", bk_s, Ctk, Stk, krot, psk, "p2_psk"))):
                            for ct in range(NCT):
                                prog.add("pe", lambda e, w=wbuf, ct=ct, ot=ot,
                                         psx=psx, xb=xb: e.matmul(
                                             psx[:],
                                             w[:, ct * 1024 + ot * 128:
                                               ct * 1024 + (ot + 1) * 128],
                                             xch[xb][:, ct * 512:(ct + 1) * 512],
                                             start=(ct == 0),
                                             stop=(ct == NCT - 1)),
                                         reads=[B_("p2_" + wnm), B_(f"p2_xch{xb}")],
                                         writes=[B_(psnm)])
                            prog.add("act", lambda e, bc=bcol, ot=ot, psx=psx,
                                     si=si, psnm=psnm: e.activation(
                                         qf[si][:], psx[:], AF.Identity,
                                         bias=bc[:, ot:ot + 1]),
                                reads=[B_(psnm), B_("c_bq_s"), B_("c_bk_s")],
                                writes=[B_(f"p2_qf{si}")])
                            qeng = "pool" if si == 0 else "sync"
                            for (d0, d1) in ((0, 32), (32, 0), (64, 96), (96, 64)):
                                prog.add(qeng, lambda e, d0=d0, d1=d1, si=si:
                                         e.dma_start(
                                             out=qsw[si][d0:d0 + 32, :],
                                             in_=qf[si][d1:d1 + 32, :]),
                                         reads=[B_(f"p2_qf{si}")],
                                         writes=[B_(f"p2_qsw{si}")], inc=16)
                            prog.add("dve", lambda e, Ct=Ct, si=si: e.tensor_mul(
                                qt1[si][:], qf[si][:], Ct[:]),
                                reads=[B_(f"p2_qf{si}"), B_("p2_tab")],
                                writes=[B_(f"p2_qt1{si}")])
                            prog.add("dve", lambda e, St=St, si=si: e.tensor_mul(
                                qt2[si][:], qsw[si][:], St[:]),
                                reads=[B_(f"p2_qsw{si}"), B_("p2_tab")],
                                writes=[B_(f"p2_qt2{si}")])
                            prog.add("dve", lambda e, si=si: e.tensor_add(
                                qbf[si][:], qt1[si][:], qt2[si][:]),
                                reads=[B_(f"p2_qt1{si}"), B_(f"p2_qt2{si}")],
                                writes=[B_(f"p2_qbf{si}")])
                            prog.add("sync", lambda e, d=dst, ot=ot, tc=tc, si=si:
                                     e.dma_start(
                                         out=d[ot * 128:(ot + 1) * 128,
                                               tc * 512:(tc + 1) * 512],
                                         in_=qbf[si][:]),
                                     reads=[B_(f"p2_qbf{si}")], writes=[B_("qkstage")])
                    for sub in range(4):
                        kb = tc * 4 + sub
                        for half in range(2):
                            vp = (sub * 2 + half) % 2
                            for ct in range(NCT):
                                prog.add("pe", lambda e, ct=ct, sub=sub, half=half,
                                         vp=vp, xb=xb: e.matmul(
                                             psv[vp][:],
                                             xch[xb][:, ct * 512 + sub * 128:
                                                 ct * 512 + (sub + 1) * 128],
                                             wv_s[:, ct * 1024 + half * 512:
                                                  ct * 1024 + (half + 1) * 512],
                                             start=(ct == 0), stop=(ct == NCT - 1)),
                                         reads=[B_("p2_wv"), B_(f"p2_xch{xb}")],
                                         writes=[B_(f"p2_psv{vp}")])
                            prog.add("dve", lambda e, half=half, vp=vp: e.tensor_add(
                                vbf[:], psv[vp][:],
                                bvb[:, half * 512:(half + 1) * 512]),
                                reads=[B_(f"p2_psv{vp}"), B_("bvb")],
                                writes=[B_("p2_vbf")])
                            for hpi in range(4):
                                hp = half * 4 + hpi
                                prog.add("sync", lambda e, hp=hp, kb=kb, hpi=hpi:
                                         e.dma_start(
                                             out=vt[hp * 128:(hp + 1) * 128,
                                                    kb * 128:(kb + 1) * 128],
                                             in_=vbf[:, hpi * 128:(hpi + 1) * 128]),
                                         reads=[B_("p2_vbf")], writes=[B_("vtstage")])
                prog.barrier()
                _emit_block(nc, prog, sems)

        if upto >= 3:
            with ExitStack() as ph:
                psb = lambda n, s, d: ph.enter_context(nc.sbuf_tensor(n, s, d))
                pps = lambda n, s: ph.enter_context(nc.psum_tensor(n, s, F32))
                qp = [psb(f"p4_qp{i}", [128, T], BF16) for i in (0, 1)]
                kp = [psb(f"p4_kp{i}", [128, T], BF16) for i in (0, 1)]
                vtp = [psb(f"p4_vtp{i}", [128, T], BF16) for i in (0, 1)]
                vpad = [psb(f"p4_vpad{i}", [128, 32 * 130], BF16) for i in (0, 1)]
                pt2 = [psb(f"p4_pt2{i}", [128, 1024], BF16) for i in (0, 1, 2)]
                ao = psb("p4_ao", [128, T], BF16)
                recA = psb("p4_recA", [1, T], F32)
                recB = psb("p4_recB", [1, T], F32)
                rsA = psb("p4_rsA", [1, T], F32)
                rsB = psb("p4_rsB", [1, T], F32)
                nb = psb("p4_nb", [128, 512], BF16)

                psS2 = [pps(f"p4_psS2{i}", [128, 1024]) for i in (0, 1, 2)]
                psPA = ph.enter_context(nc.psum_tensor("p4_psPA", [65, 512], F32))
                psPB = ph.enter_context(nc.psum_tensor("p4_psPB", [65, 512], F32))

                def load_pair(hp):
                    hb = hp % 2
                    prog.add("sync", lambda e, hp=hp, hb=hb: e.dma_start(
                        out=qp[hb][:], in_=qrot[hp * 128:(hp + 1) * 128, :]),
                        reads=[B_("qkstage")], writes=[B_(f"p4_qp{hb}")])
                    prog.add("sync", lambda e, hp=hp, hb=hb: e.dma_start(
                        out=kp[hb][:], in_=krot[hp * 128:(hp + 1) * 128, :]),
                        reads=[B_("qkstage")], writes=[B_(f"p4_kp{hb}")])
                    prog.add("sync", lambda e, hp=hp, hb=hb: e.dma_start(
                        out=vtp[hb][:], in_=vt[hp * 128:(hp + 1) * 128, :]),
                        reads=[B_("vtstage")], writes=[B_(f"p4_vtp{hb}")])
                    prog.add("dve", lambda e, hb=hb: e.memset(vpad[hb][:], 1.0),
                             writes=[B_(f"p4_vpad{hb}")])
                    for j0 in (0, 64):
                        prog.add("dve", lambda e, j0=j0, hb=hb: e.tensor_copy(
                            vpad[hb][:].rearrange("p (kb c) -> p kb c", c=130)[
                                :, :, j0 + (j0 // 64):j0 + (j0 // 64) + 64],
                            vtp[hb][:].rearrange("p (kb j) -> p kb j", j=128)[
                                :, :, j0:j0 + 64]),
                            reads=[B_(f"p4_vtp{hb}")], writes=[B_(f"p4_vpad{hb}")])

                load_pair(0)
                for hp in range(8):
                    hb = hp % 2
                    if hp + 1 < 8:
                        load_pair(hp + 1)
                    for qc in range(8):
                        nkt = (qc + 1) * 4

                        def emit_scores(ktb, qc=qc, hb=hb):
                            kp_ = ktb % 3
                            prog.add("pe", lambda e, ktb=ktb, qc=qc, kp_=kp_,
                                     hb=hb: e.matmul(
                                         psS2[kp_][:, 0:512],
                                         kp[hb][0:64, ktb * 128:(ktb + 1) * 128],
                                         qp[hb][0:64, qc * 512:(qc + 1) * 512],
                                         start=True, stop=True,
                                         tile_position=(0, 0)),
                                     reads=[B_(f"p4_kp{hb}"), B_(f"p4_qp{hb}")],
                                     writes=[B_(f"p4_psS2{kp_}")])
                            prog.add("pe", lambda e, ktb=ktb, qc=qc, kp_=kp_,
                                     hb=hb: e.matmul(
                                         psS2[kp_][:, 512:1024],
                                         kp[hb][64:128, ktb * 128:(ktb + 1) * 128],
                                         qp[hb][64:128, qc * 512:(qc + 1) * 512],
                                         start=True, stop=True,
                                         tile_position=(64, 0)),
                                     reads=[B_(f"p4_kp{hb}"), B_(f"p4_qp{hb}")],
                                     writes=[B_(f"p4_psS2{kp_}")])
                            prog.add("act", lambda e, kp_=kp_: e.activation(
                                pt2[kp_][:], psS2[kp_][:], AF.Exp),
                                reads=[B_(f"p4_psS2{kp_}")],
                                writes=[B_(f"p4_pt2{kp_}")])
                            if ktb >= qc * 4:
                                o = ktb - qc * 4
                                for hh in range(2):
                                    prog.add("dve", lambda e, o=o, kp_=kp_, hh=hh:
                                             e.tensor_mul(
                                                 pt2[kp_][:, hh * 512:
                                                          (hh + 1) * 512],
                                                 pt2[kp_][:, hh * 512:
                                                          (hh + 1) * 512],
                                                 mask_s[:, o * 512:(o + 1) * 512]),
                                             reads=[B_("mask_s")],
                                             writes=[B_(f"p4_pt2{kp_}")])

                        def emit_pv(ktb, qc=qc, nkt=nkt, hb=hb):
                            kp_ = ktb % 3
                            prog.add("pe", lambda e, ktb=ktb, nkt=nkt, kp_=kp_,
                                     hb=hb: e.matmul(
                                         psPA[:],
                                         vpad[hb][:, ktb * 130:ktb * 130 + 65],
                                         pt2[kp_][:, 0:512], start=(ktb == 0),
                                         stop=(ktb == nkt - 1)),
                                     reads=[B_(f"p4_vpad{hb}"),
                                            B_(f"p4_pt2{kp_}")],
                                     writes=[B_("p4_psPA")])
                            prog.add("pe", lambda e, ktb=ktb, nkt=nkt, kp_=kp_,
                                     hb=hb: e.matmul(
                                         psPB[:],
                                         vpad[hb][:, ktb * 130 + 65:
                                                  ktb * 130 + 130],
                                         pt2[kp_][:, 512:1024], start=(ktb == 0),
                                         stop=(ktb == nkt - 1)),
                                     reads=[B_(f"p4_vpad{hb}"),
                                            B_(f"p4_pt2{kp_}")],
                                     writes=[B_("p4_psPB")])

                        emit_scores(0)
                        if nkt > 1:
                            emit_scores(1)
                        for ktb in range(2, nkt):
                            emit_scores(ktb)
                            emit_pv(ktb - 2)
                        emit_pv(nkt - 2)
                        emit_pv(nkt - 1)
                        prog.add("act", lambda e, qc=qc: e.activation(
                            ao[0:64, qc * 512:(qc + 1) * 512], psPA[0:64, :],
                            AF.Copy),
                            reads=[B_("p4_psPA")], writes=[B_("p4_ao")])
                        prog.add("act", lambda e, qc=qc: e.activation(
                            ao[64:128, qc * 512:(qc + 1) * 512], psPB[0:64, :],
                            AF.Copy),
                            reads=[B_("p4_psPB")], writes=[B_("p4_ao")])
                        prog.add("act", lambda e, qc=qc: e.activation(
                            rsA[0:1, qc * 512:(qc + 1) * 512], psPA[64:65, :],
                            AF.Copy),
                            reads=[B_("p4_psPA")], writes=[B_("p4_rsA")])
                        prog.add("act", lambda e, qc=qc: e.activation(
                            rsB[0:1, qc * 512:(qc + 1) * 512], psPB[64:65, :],
                            AF.Copy),
                            reads=[B_("p4_psPB")], writes=[B_("p4_rsB")])
                        prog.add("dve", lambda e, qc=qc: e.reciprocal(
                            recA[0:1, qc * 512:(qc + 1) * 512],
                            rsA[0:1, qc * 512:(qc + 1) * 512]),
                            reads=[B_("p4_rsA")], writes=[B_("p4_recA")])
                        prog.add("dve", lambda e, qc=qc: e.reciprocal(
                            recB[0:1, qc * 512:(qc + 1) * 512],
                            rsB[0:1, qc * 512:(qc + 1) * 512]),
                            reads=[B_("p4_rsB")], writes=[B_("p4_recB")])
                    for qc in range(8):
                        for (rec, rnm, lo) in ((recA, "p4_recA", 0),
                                               (recB, "p4_recB", 64)):
                            prog.add("pe", lambda e, r=rec, qc=qc: e.matmul(
                                psPA[0:64, :], ones1[0:1, 0:64],
                                r[0:1, qc * 512:(qc + 1) * 512],
                                start=True, stop=True),
                                reads=[B_(rnm), B_("ones1")], writes=[B_("p4_psPA")])
                            prog.add("dve", lambda e, lo=lo: e.tensor_copy(
                                nb[lo:lo + 64, :], psPA[0:64, :]),
                                reads=[B_("p4_psPA")], writes=[B_("p4_nb")])
                            prog.add("dve", lambda e, lo=lo, qc=qc: e.tensor_mul(
                                ao[lo:lo + 64, qc * 512:(qc + 1) * 512],
                                ao[lo:lo + 64, qc * 512:(qc + 1) * 512],
                                nb[lo:lo + 64, :]),
                                reads=[B_("p4_nb")], writes=[B_("p4_ao")])
                    prog.add("sync", lambda e, hp=hp: e.dma_start(
                        out=attn_my[hp * 128:(hp + 1) * 128, :], in_=ao[:]),
                        reads=[B_("p4_ao")], writes=[B_("attn_my")])
                    drain_build(330)
                drain_build(10 ** 9)
                prog.add("pool", lambda e: e.collective_compute(
                    "AllGather", ALU.bypass, [[0, 1], [2, 3], [4, 5], [6, 7]],
                    ins=[attn_my.opt()], outs=[attn_g.opt()]),
                    reads=[B_("attn_my")], writes=[B_("attn_g")], inc=1)
                prog.add("pool", lambda e: e.collective_compute(
                    "AllGather", ALU.bypass, [[0, 1, 2, 3, 4, 5, 6, 7]],
                    ins=[wsh.opt()], outs=[wg8.opt()]),
                    reads=[B_("wsh")], writes=[B_("wg8")], inc=1)
                prog.barrier()
                _emit_block(nc, prog, sems)

        def ln_inline(tag, xo_t, xo_bufs, psA, psB, bA, bB, rows, dst_ap, tc,
                      psR, psM, bR, bM):
            prog.add("dve", lambda e: e.tensor_scalar_mul(
                rows["m"][0][:], psA[:], 1.0 / D),
                reads=[bA], writes=[rows["m"][1]])
            prog.add("dve", lambda e: e.tensor_scalar_mul(
                rows["q"][0][:], psB[:], 1.0 / D),
                reads=[bB], writes=[rows["q"][1]])
            prog.add("dve", lambda e: e.scalar_tensor_tensor(
                rows["r"][0][:], rows["m"][0][:], 1.0, rows["m"][0][:],
                ALU.mult, ALU.mult),
                reads=[rows["m"][1]], writes=[rows["r"][1]])
            prog.add("dve", lambda e: e.tensor_sub(
                rows["q"][0][:], rows["q"][0][:], rows["r"][0][:]),
                reads=[rows["q"][1], rows["r"][1]], writes=[rows["q"][1]])
            prog.add("act", lambda e: e.activation(
                rows["r"][0][:], rows["q"][0][:], AF.Ln, bias=cv[0:1, 0:1]),
                reads=[rows["q"][1], B_("c_cv")], writes=[rows["r"][1]])
            prog.add("act", lambda e: e.activation(
                rows["r"][0][:], rows["r"][0][:], AF.Exp, scale=cv[0:1, 1:2]),
                reads=[rows["r"][1]], writes=[rows["r"][1]])
            prog.add("dve", lambda e: e.scalar_tensor_tensor(
                rows["mb"][0][:], rows["m"][0][:], -1.0, rows["r"][0][:],
                ALU.mult, ALU.mult),
                reads=[rows["m"][1], rows["r"][1]], writes=[rows["mb"][1]])
            prog.add("pe", lambda e: e.matmul(
                psR[:], ones1[:], rows["r"][0][:], start=True, stop=True),
                reads=[rows["r"][1], B_("ones1")], writes=[bR])
            prog.add("pe", lambda e: e.matmul(
                psM[:], ones1[:], rows["mb"][0][:], start=True, stop=True),
                reads=[rows["mb"][1], B_("ones1")], writes=[bM])
            for ct in range(NCT):
                prog.add("dve", lambda e, ct=ct: e.tensor_mul(
                    rows["t1"][0][:], xo_t[ct][:], psR[:]),
                    reads=[xo_bufs[ct], bR], writes=[rows["t1"][1]])
                prog.add("dve", lambda e, ct=ct: e.tensor_add(
                    rows["xn"][0][:], rows["t1"][0][:], psM[:]),
                    reads=[rows["t1"][1], bM], writes=[rows["xn"][1]])
                prog.add("sync", lambda e, ct=ct, tc=tc: e.dma_start(
                    out=dst_ap[ct * 128:(ct + 1) * 128, tc * 512:(tc + 1) * 512],
                    in_=rows["xn"][0][:]),
                    reads=[rows["xn"][1]], writes=[B_(tag + "_dst")])

        if upto >= 4:
            with ExitStack() as ph:
                psb = lambda n, s, d: ph.enter_context(nc.sbuf_tensor(n, s, d))
                pps = lambda n, s: ph.enter_context(nc.psum_tensor(n, s, F32))
                wo_s = psb("p5_wo", [128, 16 * 2048], BF16)
                ac_s = psb("p5_ac", [128, 16 * 512], BF16)
                ah0 = psb("p5_ah0", [128, 512], BF16)
                ah1 = psb("p5_ah1", [128, 512], BF16)
                aht = psb("p5_aht", [128, 512], BF16)
                x0t = psb("p5_x0t", [128, 512], F32)
                xo_t = [psb(f"p5_xo{i}", [128, 512], F32) for i in range(NCT)]
                sq = psb("p5_sq", [128, 512], F32)
                rows = {k: (psb(f"p5_r{k}", [1, 512], F32), B_(f"p5_r{k}"))
                        for k in ("m", "q", "r", "mb")}
                rows["t1"] = (psb("p5_t1", [128, 512], F32), B_("p5_t1"))
                rows["xn"] = (psb("p5_xn", [128, 512], BF16), B_("p5_xn"))
                pso = [pps(f"p5_pso{i}", [128, 512]) for i in (0, 1)]
                psA = pps("p5_psA", [1, 512])
                psB = pps("p5_psB", [1, 512])
                psR = pps("p5_psR", [128, 512])
                psM = pps("p5_psM", [128, 512])
                prog.add("sync", lambda e: e.dma_start(
                    out=wo_s[:].rearrange("p (ct o) -> p ct o", o=2048),
                    in_=wo_bf.rearrange("(ct p) o -> p ct o", p=128)),
                    reads=[B_("wobf")], writes=[B_("p5_wo")])
                for tc in range(4):
                    for ct in range(NCT):
                        for half, dstb in ((0, ah0), (1, ah1)):
                            prog.add("sync", lambda e, ct=ct, tc=tc, half=half,
                                     d=dstb: e.dma_start(
                                         out=d[:],
                                         in_=attn_g[ct * 128:(ct + 1) * 128,
                                                    half * 2048 + tc * 512:
                                                    half * 2048 + (tc + 1) * 512]),
                                     reads=[B_("attn_g")],
                                     writes=[B_("p5_ah%d" % half)])
                        prog.add("dve", lambda e: e.tensor_scalar(
                            aht[:], ah1[:], sel1c[:, 0:1], None, ALU.mult),
                            reads=[B_("p5_ah1"), B_("c_sel")], writes=[B_("p5_aht")])
                        prog.add("dve", lambda e, ct=ct: e.scalar_tensor_tensor(
                            ac_s[:, ct * 512:(ct + 1) * 512], ah0[:],
                            sel0c[:, 0:1], aht[:], ALU.mult, ALU.add),
                            reads=[B_("p5_ah0"), B_("p5_aht"), B_("c_sel")],
                            writes=[B_("p5_ac")])
                    for ot in range(NCT):
                        op_ = ot % 2
                        for ct in range(NCT):
                            prog.add("pe", lambda e, ot=ot, ct=ct, op_=op_: e.matmul(
                                pso[op_][:],
                                wo_s[:, ct * 2048 + ot * 128:
                                     ct * 2048 + (ot + 1) * 128],
                                ac_s[:, ct * 512:(ct + 1) * 512],
                                start=(ct == 0), stop=(ct == NCT - 1)),
                                reads=[B_("p5_wo"), B_("p5_ac")],
                                writes=[B_(f"p5_pso{op_}")])
                        prog.add("act", lambda e, ot=ot, op_=op_: e.activation(
                            xo_t[ot][:], pso[op_][:], AF.Identity,
                            bias=bo_s[:, ot:ot + 1]),
                            reads=[B_(f"p5_pso{op_}"), B_("c_bo_s")],
                            writes=[B_(f"p5_xo{ot}")])
                        prog.add("sync", lambda e, ot=ot, tc=tc: e.dma_start(
                            out=x0t[:], in_=x0[ot * 128:(ot + 1) * 128,
                                              tc * 512:(tc + 1) * 512]),
                            writes=[B_("p5_x0t")])
                        prog.add("dve", lambda e, ot=ot: e.tensor_add(
                            xo_t[ot][:], xo_t[ot][:], x0t[:]),
                            reads=[B_("p5_x0t")], writes=[B_(f"p5_xo{ot}")])
                        prog.add("sync", lambda e, ot=ot, tc=tc: e.dma_start(
                            out=x1[ot * 128:(ot + 1) * 128,
                                   tc * 512:(tc + 1) * 512],
                            in_=xo_t[ot][:]),
                            reads=[B_(f"p5_xo{ot}")], writes=[B_("x1")])
                        prog.add("pe", lambda e, ot=ot: e.matmul(
                            psA[:], onesC[:], xo_t[ot][:], start=(ot == 0),
                            stop=(ot == NCT - 1)),
                            reads=[B_(f"p5_xo{ot}"), B_("onesC")],
                            writes=[B_("p5_psA")])
                        prog.add("act", lambda e, ot=ot: e.activation(
                            sq[:], xo_t[ot][:], AF.Square),
                            reads=[B_(f"p5_xo{ot}")], writes=[B_("p5_sq")])
                        prog.add("pe", lambda e, ot=ot: e.matmul(
                            psB[:], onesC[:], sq[:], start=(ot == 0),
                            stop=(ot == NCT - 1)),
                            reads=[B_("p5_sq"), B_("onesC")], writes=[B_("p5_psB")])
                    ln_inline("ln2", xo_t, [B_(f"p5_xo{i}") for i in range(NCT)],
                              psA, psB, B_("p5_psA"), B_("p5_psB"), rows, xn2, tc,
                              psR, psM, B_("p5_psR"), B_("p5_psM"))
                prog.barrier()
                _emit_block(nc, prog, sems)

        def ffn(tag, xn_ap, xres_ap, xo_ap, xn_next_ap):
            GOFF = 0
            UOFF = D * 768
            DOFF = 2 * D * 768
            with ExitStack() as ph:
                psb = lambda n, s, d: ph.enter_context(nc.sbuf_tensor(n, s, d))
                xn_s = psb(f"{tag}_xn", [128, 16 * 2048], BF16)
                wg_sb = [psb(f"{tag}_wg{i}", [128, 16 * 768], BF16) for i in (0, 1)]
                wu_sb = [psb(f"{tag}_wu{i}", [128, 16 * 768], BF16) for i in (0, 1)]
                gs = [psb(f"{tag}_gs{i}", [128, 512], BF16) for i in (0, 1)]
                us = [psb(f"{tag}_us{i}", [128, 512], BF16) for i in (0, 1)]
                ht = [psb(f"{tag}_ht{i}", [128, 512], BF16) for i in (0, 1)]
                with ExitStack() as phh:
                    ppsg = [phh.enter_context(
                        nc.psum_tensor(f"{tag}_psg{i}", [128, 512], F32))
                        for i in (0, 1)]
                    ppsu = [phh.enter_context(
                        nc.psum_tensor(f"{tag}_psu{i}", [128, 512], F32))
                        for i in (0, 1)]
                    for ct in range(NCT):
                        prog.add("sync", lambda e, ct=ct: e.dma_start(
                            out=xn_s[:, ct * 2048:(ct + 1) * 2048],
                            in_=xn_ap[ct * 128:(ct + 1) * 128, :]),
                            reads=[B_(tag + "_xnsrc")], writes=[B_(f"{tag}_xn")])
                    for r in range(8):
                        pb = r % 2
                        prog.add("sync", lambda e, r=r, pb=pb: e.dma_start(
                            out=wg_sb[pb][:],
                            in_=wg8[r, GOFF:GOFF + D * 768].rearrange(
                                "(p ci) -> p ci", p=128)),
                            reads=[B_("wg8")], writes=[B_(f"{tag}_wg{pb}")])
                        prog.add("sync", lambda e, r=r, pb=pb: e.dma_start(
                            out=wu_sb[pb][:],
                            in_=wg8[r, UOFF:UOFF + D * 768].rearrange(
                                "(p ci) -> p ci", p=128)),
                            reads=[B_("wg8")], writes=[B_(f"{tag}_wu{pb}")])
                        for il in range(6):
                            ig = r * 6 + il
                            if ig >= NIT:
                                continue
                            for sub in range(4):
                                gp = sub % 2
                                for ct in range(NCT):
                                    prog.add("pe", lambda e, pb=pb, ct=ct, il=il,
                                             sub=sub, gp=gp: e.matmul(
                                                 ppsg[gp][:],
                                                 wg_sb[pb][:, ct * 768 + il * 128:
                                                           ct * 768 + (il + 1) * 128],
                                                 xn_s[:, ct * 2048 + sub * 512:
                                                      ct * 2048 + (sub + 1) * 512],
                                                 start=(ct == 0),
                                                 stop=(ct == NCT - 1)),
                                             reads=[B_(f"{tag}_wg{pb}"),
                                                    B_(f"{tag}_xn")],
                                             writes=[B_(f"{tag}_psg{gp}")])
                                prog.add("act", lambda e, gp=gp: e.activation(
                                    gs[gp][:], ppsg[gp][:], AF.Silu),
                                    reads=[B_(f"{tag}_psg{gp}")],
                                    writes=[B_(f"{tag}_gs{gp}")])
                                for ct in range(NCT):
                                    prog.add("pe", lambda e, pb=pb, ct=ct, il=il,
                                             sub=sub, gp=gp: e.matmul(
                                                 ppsu[gp][:],
                                                 wu_sb[pb][:, ct * 768 + il * 128:
                                                           ct * 768 + (il + 1) * 128],
                                                 xn_s[:, ct * 2048 + sub * 512:
                                                      ct * 2048 + (sub + 1) * 512],
                                                 start=(ct == 0),
                                                 stop=(ct == NCT - 1)),
                                             reads=[B_(f"{tag}_wu{pb}"),
                                                    B_(f"{tag}_xn")],
                                             writes=[B_(f"{tag}_psu{gp}")])
                                prog.add("act", lambda e, gp=gp: e.activation(
                                    us[gp][:], ppsu[gp][:], AF.Copy),
                                    reads=[B_(f"{tag}_psu{gp}")],
                                    writes=[B_(f"{tag}_us{gp}")])
                                prog.add("dve", lambda e, gp=gp: e.tensor_mul(
                                    ht[gp][:], gs[gp][:], us[gp][:]),
                                    reads=[B_(f"{tag}_gs{gp}"), B_(f"{tag}_us{gp}")],
                                    writes=[B_(f"{tag}_ht{gp}")])
                                prog.add("sync", lambda e, ig=ig, sub=sub, gp=gp:
                                         e.dma_start(
                                             out=hbuf[ig * 128:(ig + 1) * 128,
                                                      sub * 512:(sub + 1) * 512],
                                             in_=ht[gp][:]),
                                         reads=[B_(f"{tag}_ht{gp}")],
                                         writes=[B_("hbuf")])
                prog.barrier()
                _emit_block(nc, prog, sems)
            final = xn_next_ap is None
            with ExitStack() as ph:
                psb = lambda n, s, d: ph.enter_context(nc.sbuf_tensor(n, s, d))
                pps = lambda n, s: ph.enter_context(nc.psum_tensor(n, s, F32))
                h_s = psb(f"{tag}_hs", [128, NIT * 1024], BF16)
                wd_sb = [psb(f"{tag}_wd{i}", [128, NIT * 256], BF16) for i in (0, 1)]
                xrt = [psb(f"{tag}_xrt{i}", [128, 1024], F32) for i in (0, 1)]
                xod = [psb(f"{tag}_xod{i}", [128, 1024], F32) for i in (0, 1)]
                psd2 = [pps(f"{tag}_psd2{i}", [128, 1024]) for i in (0, 1)]
                for sp in range(2):
                    for part in range(4):
                        prog.add("sync", lambda e, sp=sp, part=part: e.dma_start(
                            out=h_s[:].rearrange("p (it t) -> p it t", t=1024)[
                                :, part * 11:min((part + 1) * 11, NIT), :],
                            in_=hbuf[:, sp * 1024:(sp + 1) * 1024].rearrange(
                                "(it p) t -> p it t", p=128)[
                                :, part * 11:min((part + 1) * 11, NIT), :]),
                            reads=[B_("hbuf")], writes=[B_(f"{tag}_hs")])
                    for r in range(8):
                        pb = r % 2
                        prog.add("sync", lambda e, r=r, pb=pb: e.dma_start(
                            out=wd_sb[pb][:],
                            in_=wg8[r, DOFF:DOFF + INTER * 256].rearrange(
                                "(p io) -> p io", p=128)),
                            reads=[B_("wg8")], writes=[B_(f"{tag}_wd{pb}")])
                        for otl in range(2):
                            ot = r * 2 + otl
                            dp = (r * 2 + otl) % 2
                            for half in range(2):
                                for it in range(NIT):
                                    prog.add("pe", lambda e, pb=pb, it=it, otl=otl,
                                             dp=dp, half=half: e.matmul(
                                                 psd2[dp][:, half * 512:
                                                          (half + 1) * 512],
                                                 wd_sb[pb][:, it * 256 + otl * 128:
                                                           it * 256 +
                                                           (otl + 1) * 128],
                                                 h_s[:, it * 1024 + half * 512:
                                                     it * 1024 + (half + 1) * 512],
                                                 start=(it == 0),
                                                 stop=(it == NIT - 1)),
                                             reads=[B_(f"{tag}_wd{pb}"),
                                                    B_(f"{tag}_hs")],
                                             writes=[B_(f"{tag}_psd2{dp}")])
                            prog.add("sync", lambda e, ot=ot, sp=sp, dp=dp:
                                     e.dma_start(
                                         out=xrt[dp][:],
                                         in_=xres_ap[ot * 128:(ot + 1) * 128,
                                                     sp * 1024:(sp + 1) * 1024]),
                                     reads=[B_(tag + "_xressrc")],
                                     writes=[B_(f"{tag}_xrt{dp}")])
                            prog.add("dve", lambda e, dp=dp: e.tensor_add(
                                xod[dp][:], psd2[dp][:], xrt[dp][:]),
                                reads=[B_(f"{tag}_psd2{dp}"), B_(f"{tag}_xrt{dp}")],
                                writes=[B_(f"{tag}_xod{dp}")])
                            prog.add("sync", lambda e, ot=ot, sp=sp, dp=dp:
                                     e.dma_start(
                                         out=xo_ap[ot * 128:(ot + 1) * 128,
                                                   sp * 1024:(sp + 1) * 1024],
                                         in_=xod[dp][:]),
                                     reads=[B_(f"{tag}_xod{dp}")],
                                     writes=[B_(tag + "_xo")])
                prog.barrier()
                _emit_block(nc, prog, sems)
            if not final:
                ln_pass(tag + "ln", xo_ap, xn_next_ap)

        if upto >= 5:
            bufs["f1_xnsrc"] = bufs.get("ln2_dst", Buf("f1_xnsrc"))
            bufs["f1_xressrc"] = bufs.get("x1", Buf("f1_xressrc"))
            ffn("f1", xn2, x1, x2, xn3)
            bufs["x2"] = bufs["f1_xo"]
            bufs["xn3_dst"] = bufs["f1ln_dst"]

        if upto >= 6:
          with ExitStack() as p7st:
            with ExitStack() as ph:
                psb = lambda n, s, d: ph.enter_context(nc.sbuf_tensor(n, s, d))
                pps = lambda n, s: ph.enter_context(nc.psum_tensor(n, s, F32))
                pin_sb = psb("p7_pin", [128, 16 * 2048], BF16)
                xn_c = psb("p7_xnc", [128, 16 * 512], BF16)
                xpf = psb("p7_xpf", [128, 512], F32)
                xpb = psb("p7_xpb", [128, 512], BF16)
                psxp = [pps(f"p7_psxp{i}", [128, 512]) for i in (0, 1)]
                prog.add("sync", lambda e: e.dma_start(
                    out=pin_sb[:].rearrange("p (ct o) -> p ct o", o=2048),
                    in_=pin_bf.rearrange("(ct p) o -> p ct o", p=128)),
                    reads=[B_("pinbf")], writes=[B_("p7_pin")])
                for tc in range(4):
                    for ct in range(NCT):
                        prog.add("sync", lambda e, ct=ct, tc=tc: e.dma_start(
                            out=xn_c[:, ct * 512:(ct + 1) * 512],
                            in_=xn3[ct * 128:(ct + 1) * 128,
                                    tc * 512:(tc + 1) * 512]),
                            reads=[B_("xn3_dst")], writes=[B_("p7_xnc")])
                    for dt in range(NCT):
                        xp_ = dt % 2
                        for ct in range(NCT):
                            prog.add("pe", lambda e, dt=dt, ct=ct, xp_=xp_: e.matmul(
                                psxp[xp_][:],
                                pin_sb[:, ct * 2048 + dt * 128:
                                       ct * 2048 + (dt + 1) * 128],
                                xn_c[:, ct * 512:(ct + 1) * 512],
                                start=(ct == 0), stop=(ct == NCT - 1)),
                                reads=[B_("p7_pin"), B_("p7_xnc")],
                                writes=[B_(f"p7_psxp{xp_}")])
                        prog.add("act", lambda e, dt=dt, xp_=xp_: e.activation(
                            xpf[:], psxp[xp_][:], AF.Identity,
                            bias=pinb_s[:, dt:dt + 1]),
                            reads=[B_(f"p7_psxp{xp_}"), B_("c_pinb_s")],
                            writes=[B_("p7_xpf")])
                        prog.add("dve", lambda e: e.tensor_copy(xpb[:], xpf[:]),
                            reads=[B_("p7_xpf")], writes=[B_("p7_xpb")])
                        prog.add("sync", lambda e, dt=dt, tc=tc: e.dma_start(
                            out=xp_my[dt * 128:(dt + 1) * 128,
                                      tc * 512:(tc + 1) * 512],
                            in_=xpb[:]),
                            reads=[B_("p7_xpb")], writes=[B_("xp_my")])
                prog.add("pool", lambda e: e.collective_compute(
                    "AllGather", ALU.bypass, [[0, 1], [2, 3], [4, 5], [6, 7]],
                    ins=[xp_my.opt()], outs=[xp_g.opt()]),
                    reads=[B_("xp_my")], writes=[B_("xp_g")], inc=1)
                xpdt = psb("p7_xpdt", [128, 512], BF16)
                dl0 = psb("p7_dl0", [16, 512], F32)
                dl1 = psb("p7_dl1", [16, 512], F32)
                dT_s = psb("p7_dTs", [16, 2048], BF16)
                psdl = pps("p7_psdl", [16, 512])
                import os as _os2
                _lvl = int(_os2.environ.get("P7B_LVL", "6"))
                for tch in range(4):
                    for dt in range(NCT):
                        prog.add("sync", lambda e, dt=dt, tch=tch: e.dma_start(
                            out=xpdt[:], in_=xp_my[dt * 128:(dt + 1) * 128,
                                                   tch * 512:(tch + 1) * 512]),
                            reads=[B_("xp_my")], writes=[B_("p7_xpdt")])
                        if _lvl < 2:
                            continue
                        prog.add("pe", lambda e, dt=dt: e.matmul(
                            psdl[:], BwT_s[:, dt * 16:(dt + 1) * 16], xpdt[:],
                            start=(dt == 0), stop=(dt == NCT - 1)),
                            reads=[B_("BwT_s"), B_("p7_xpdt")],
                            writes=[B_("p7_psdl")])
                    if _lvl < 3:
                        continue
                    prog.add("act", lambda e: e.activation(dl0[:], psdl[:], AF.Abs),
                             reads=[B_("p7_psdl")], writes=[B_("p7_dl0")])
                    prog.add("act", lambda e: e.activation(
                        dl0[:], dl0[:], AF.Exp, scale=-1.0),
                        reads=[B_("p7_dl0")], writes=[B_("p7_dl0")])
                    prog.add("act", lambda e: e.activation(
                        dl0[:], dl0[:], AF.Ln, bias=1.0),
                        reads=[B_("p7_dl0")], writes=[B_("p7_dl0")])
                    if _lvl < 4:
                        continue
                    prog.add("dve", lambda e: e.tensor_scalar_max(
                        dl1[:], psdl[:], 0.0),
                        reads=[B_("p7_psdl")], writes=[B_("p7_dl1")])
                    if _lvl < 5:
                        continue
                    prog.add("dve", lambda e, tch=tch: e.tensor_add(
                        dT_s[:, tch * 512:(tch + 1) * 512], dl0[:], dl1[:]),
                        reads=[B_("p7_dl0"), B_("p7_dl1")], writes=[B_("p7_dTs")])
                if int(_os2.environ.get("P7B_LVL", "6")) >= 6:
                    prog.add("sync", lambda e: e.dma_start(out=dT_my, in_=dT_s[:]),
                             reads=[B_("p7_dTs")], writes=[B_("dT_my")])
                import os as _os
                if _os.environ.get("P7B_AG", "1") == "1":
                    prog.add("pool", lambda e: e.collective_compute(
                        "AllGather", ALU.bypass, [[0, 1], [2, 3], [4, 5], [6, 7]],
                        ins=[dT_my.opt()], outs=[dT_g.opt()]),
                        reads=[B_("dT_my")], writes=[B_("dT_g")], inc=1)
                prog.barrier()
                _emit_block(nc, prog, sems)
            if p7sub >= 3:
              with ExitStack() as ph:
                psb = lambda n, s, d: ph.enter_context(nc.sbuf_tensor(n, s, d))
                pps = lambda n, s: ph.enter_context(nc.psum_tensor(n, s, F32))
                dr0 = psb("p7_dr0", [1, 2048], BF16)
                dr1 = psb("p7_dr1", [1, 2048], BF16)
                drt = psb("p7_drt", [1, 2048], BF16)
                drc = psb("p7_drc", [1, 2048], BF16)
                dbc = psb("p7_dbc", [128, 8 * 2048], BF16)
                xpr = [psb(f"p7_xpr{i}", [128, 2048], BF16) for i in (0, 1)]
                ut = {en: psb(f"p7_ut_{en}", [128, 2048], BF16)
                      for en in ("dve", "pool")}
                htl = {en: psb(f"p7_htl_{en}", [128, 2048], BF16)
                       for en in ("dve", "pool")}
                carry = psb("p7_carry", [128, 128], F32)
                G_s = psb("p7_Gs", [8, T], BF16)
                psBC = pps("p7_psBC", [128, 512])
                psG = [pps(f"p7_psG{i}", [8, 512]) for i in range(4)]
                for th in range(2):
                    for s in range(8):
                        prog.add("sync", lambda e, s=s, th=th: e.dma_start(
                            out=dr0[:], in_=dT_g[th * 16 + s:th * 16 + s + 1, :]),
                            reads=[B_("dT_g")], writes=[B_("p7_dr0")])
                        prog.add("sync", lambda e, s=s, th=th: e.dma_start(
                            out=dr1[:], in_=dT_g[th * 16 + 8 + s:
                                                 th * 16 + 8 + s + 1, :]),
                            reads=[B_("dT_g")], writes=[B_("p7_dr1")])
                        prog.add("dve", lambda e: e.tensor_scalar(
                            drt[:], dr1[:], sel1c[0:1, 0:1], None, ALU.mult),
                            reads=[B_("p7_dr1"), B_("c_sel")], writes=[B_("p7_drt")])
                        prog.add("dve", lambda e: e.scalar_tensor_tensor(
                            drc[:], dr0[:], sel0c[0:1, 0:1], drt[:],
                            ALU.mult, ALU.add),
                            reads=[B_("p7_dr0"), B_("p7_drt"), B_("c_sel")],
                            writes=[B_("p7_drc")])
                        for ch in range(4):
                            prog.add("pe", lambda e, ch=ch: e.matmul(
                                psBC[:], ones1b[:],
                                drc[0:1, ch * 512:(ch + 1) * 512],
                                start=True, stop=True),
                                reads=[B_("p7_drc"), B_("ones1b")],
                                writes=[B_("p7_psBC")])
                            prog.add("dve", lambda e, s=s, ch=ch: e.tensor_copy(
                                dbc[:, s * 2048 + ch * 512:
                                    s * 2048 + (ch + 1) * 512], psBC[:]),
                                reads=[B_("p7_psBC")], writes=[B_("p7_dbc")])
                    for dt in range(NCT):
                        pb = dt % 2
                        prog.add("sync", lambda e, dt=dt, th=th, pb=pb:
                                 e.dma_start(
                                     out=xpr[pb][:],
                                     in_=xp_g[th * D + dt * 128:
                                              th * D + (dt + 1) * 128, :]),
                                 reads=[B_("xp_g")],
                                 writes=[B_(f"p7_xpr{pb}")])
                        for s in range(8):
                            en = "dve" if s % 2 == 0 else "pool"
                            prog.add(en, lambda e, s=s, pb=pb, en=en: e.tensor_mul(
                                ut[en][:], xpr[pb][:],
                                dbc[:, s * 2048:(s + 1) * 2048]),
                                reads=[B_(f"p7_xpr{pb}"), B_("p7_dbc")],
                                writes=[B_(f"p7_ut_{en}")])
                            col = dt * 8 + s
                            if th == 0:
                                prog.add("dve", lambda e, col=col, en=en:
                                         e.tensor_tensor_scan(
                                             htl[en][:],
                                             sigA[:, col:col + 1].broadcast_to(
                                                 [128, 2048]),
                                             ut[en][:], 0.0, ALU.mult, ALU.add),
                                         reads=[B_(f"p7_ut_{en}"), B_("sigA")],
                                         writes=[B_(f"p7_htl_{en}")])
                                prog.add("dve", lambda e, col=col, en=en:
                                         e.tensor_copy(
                                             carry[:, col:col + 1],
                                             htl[en][:, 2047:2048]),
                                         reads=[B_(f"p7_htl_{en}")],
                                         writes=[B_("p7_carry")])
                            else:
                                prog.add("dve", lambda e, col=col, en=en:
                                         e.tensor_tensor_scan(
                                             htl[en][:],
                                             sigA[:, col:col + 1].broadcast_to(
                                                 [128, 2048]),
                                             ut[en][:], carry[:, col:col + 1],
                                             ALU.mult, ALU.add),
                                         reads=[B_(f"p7_ut_{en}"), B_("sigA"),
                                                B_("p7_carry")],
                                         writes=[B_(f"p7_htl_{en}")])
                            for ch in range(4):
                                prog.add("pe", lambda e, s=s, ch=ch, dt=dt, en=en:
                                         e.matmul(
                                             psG[ch][:],
                                             E_s[:, s * 8:(s + 1) * 8],
                                             htl[en][:, ch * 512:(ch + 1) * 512],
                                             start=(s == 0 and dt == 0),
                                             stop=(s == 7 and dt == NCT - 1)),
                                         reads=[B_(f"p7_htl_{en}"), B_("E_s")],
                                         writes=[B_(f"p7_psG{ch}")])
                    for ch in range(4):
                        prog.add("dve", lambda e, th=th, ch=ch:
                                 e.tensor_copy(
                                     G_s[:, th * 2048 + ch * 512:
                                         th * 2048 + (ch + 1) * 512],
                                     psG[ch][:]),
                                 reads=[B_(f"p7_psG{ch}")], writes=[B_("p7_Gs")])
                prog.add("sync", lambda e: e.dma_start(out=G_my, in_=G_s[:]),
                         reads=[B_("p7_Gs")], writes=[B_("G_my")])
                prog.add("pool", lambda e: e.collective_compute(
                    "AllGather", ALU.bypass, [[0, 1], [2, 3], [4, 5], [6, 7]],
                    ins=[G_my.opt()], outs=[G_g.opt()]),
                    reads=[B_("G_my")], writes=[B_("G_g")], inc=1)
                prog.barrier()
                _emit_block(nc, prog, sems)
            if p7sub >= 4:
              y_s = p7st.enter_context(
                nc.sbuf_tensor("p7_ys", [128, 16 * 2048], BF16))
              with ExitStack() as ph:
                  psb = lambda n, s, d: ph.enter_context(nc.sbuf_tensor(n, s, d))
                  pps = lambda n, s: ph.enter_context(nc.psum_tensor(n, s, F32))
                  gl0 = psb("p7_gl0", [16, 512], BF16)
                  gl1 = psb("p7_gl1", [16, 512], BF16)
                  glt = psb("p7_glt", [16, 512], BF16)
                  G_loc = psb("p7_gloc", [16, 2048], BF16)
                  xpd = psb("p7_xpd", [128, 512], F32)
                  xpdt2 = psb("p7_xpdt2", [128, 2048], BF16)
                  psy = pps("p7_psy", [128, 512])
                  for ch in range(4):
                      for half, dstb in ((0, gl0), (1, gl1)):
                          prog.add("sync", lambda e, ch=ch, half=half, d=dstb:
                                   e.dma_start(
                                       out=d[:],
                                       in_=G_g[:, half * 2048 + ch * 512:
                                               half * 2048 + (ch + 1) * 512]),
                                   reads=[B_("G_g")], writes=[B_("p7_gl%d" % half)])
                      prog.add("dve", lambda e: e.tensor_scalar(
                          glt[:], gl1[:], sel1c[0:16, 0:1], None, ALU.mult),
                          reads=[B_("p7_gl1"), B_("c_sel")], writes=[B_("p7_glt")])
                      prog.add("dve", lambda e, ch=ch: e.scalar_tensor_tensor(
                          G_loc[:, ch * 512:(ch + 1) * 512], gl0[:],
                          sel0c[0:16, 0:1], glt[:], ALU.mult, ALU.add),
                          reads=[B_("p7_gl0"), B_("p7_glt"), B_("c_sel")],
                          writes=[B_("p7_gloc")])
                  for dt in range(NCT):
                      prog.add("sync", lambda e, dt=dt: e.dma_start(
                          out=xpdt2[:], in_=xp_my[dt * 128:(dt + 1) * 128, :]),
                          reads=[B_("xp_my")], writes=[B_("p7_xpdt2")])
                      for ch in range(4):
                          prog.add("pe", lambda e, dt=dt, ch=ch: e.matmul(
                              psy[:], CwT_s[:, dt * 128:(dt + 1) * 128],
                              G_loc[:, ch * 512:(ch + 1) * 512],
                              start=True, stop=True),
                              reads=[B_("CwT_s"), B_("p7_gloc")], writes=[B_("p7_psy")])
                          prog.add("dve", lambda e, dt=dt, ch=ch: e.tensor_scalar(
                              xpd[:], xpdt2[:, ch * 512:(ch + 1) * 512],
                              D_s[:, dt:dt + 1], None, ALU.mult),
                              reads=[B_("p7_xpdt2"), B_("c_D_s")], writes=[B_("p7_xpd")])
                          prog.add("dve", lambda e, dt=dt, ch=ch: e.tensor_add(
                              y_s[:, dt * 2048 + ch * 512:dt * 2048 + (ch + 1) * 512],
                              psy[:], xpd[:]),
                              reads=[B_("p7_psy"), B_("p7_xpd")], writes=[B_("p7_ys")])
                  prog.barrier()
                  _emit_block(nc, prog, sems)
              with ExitStack() as ph:
                  psb = lambda n, s, d: ph.enter_context(nc.sbuf_tensor(n, s, d))
                  pps = lambda n, s: ph.enter_context(nc.psum_tensor(n, s, F32))
                  pout_sb = psb("p7_pout", [128, 16 * 2048], BF16)
                  x2t = psb("p7_x2t", [128, 512], F32)
                  xn3t = psb("p7_xn3t", [128, 512], BF16)
                  xn3f = psb("p7_xn3f", [128, 512], F32)
                  xo_t = [psb(f"p7_xo{i}", [128, 512], F32) for i in range(NCT)]
                  sq = psb("p7_sq", [128, 512], F32)
                  rows = {k: (psb(f"p7_r{k}", [1, 512], F32), B_(f"p7_r{k}"))
                          for k in ("m", "q", "r", "mb")}
                  rows["t1"] = (psb("p7_t1", [128, 512], F32), B_("p7_t1"))
                  rows["xn"] = (psb("p7_xnn", [128, 512], BF16), B_("p7_xnn"))
                  psp = [pps(f"p7_psp{i}", [128, 512]) for i in (0, 1)]
                  psA = pps("p7_psA2", [1, 512])
                  psB = pps("p7_psB2", [1, 512])
                  psR = pps("p7_psR2", [128, 512])
                  psM = pps("p7_psM2", [128, 512])
                  prog.add("sync", lambda e: e.dma_start(
                      out=pout_sb[:].rearrange("p (ct o) -> p ct o", o=2048),
                      in_=pout_bf.rearrange("(ct p) o -> p ct o", p=128)),
                      reads=[B_("poutbf")], writes=[B_("p7_pout")])
                  if DBG7:
                      prog.add("sync", lambda e: e.dma_start(
                          out=dbg_pout, in_=pout_sb[:]),
                          reads=[B_("p7_pout")], writes=[B_("dbgp")])
                      prog.add("sync", lambda e: e.dma_start(
                          out=dbg_y, in_=y_s[:]),
                          reads=[B_("p7_ys")], writes=[B_("dbgy")])
                  for tc in range(4):
                      for ot in range(NCT):
                          op_ = ot % 2
                          for dt in range(NCT):
                              prog.add("pe", lambda e, ot=ot, dt=dt, tc=tc,
                                       op_=op_: e.matmul(
                                           psp[op_][:],
                                           pout_sb[:, dt * 2048 + ot * 128:
                                                   dt * 2048 + (ot + 1) * 128],
                                           y_s[:, dt * 2048 + tc * 512:
                                               dt * 2048 + (tc + 1) * 512],
                                           start=(dt == 0), stop=(dt == NCT - 1)),
                                       reads=[B_("p7_pout"), B_("p7_ys")],
                                       writes=[B_(f"p7_psp{op_}")])
                          prog.add("act", lambda e, ot=ot, op_=op_: e.activation(
                              xo_t[ot][:], psp[op_][:], AF.Identity,
                              bias=poutb_s[:, ot:ot + 1]),
                              reads=[B_(f"p7_psp{op_}"), B_("c_poutb_s")],
                              writes=[B_(f"p7_xo{ot}")])
                          prog.add("sync", lambda e, ot=ot, tc=tc: e.dma_start(
                              out=xn3t[:], in_=xn3[ot * 128:(ot + 1) * 128,
                                                 tc * 512:(tc + 1) * 512]),
                              reads=[B_("xn3_dst")], writes=[B_("p7_xn3t")])
                          prog.add("dve", lambda e: e.tensor_copy(xn3f[:], xn3t[:]),
                                   reads=[B_("p7_xn3t")], writes=[B_("p7_xn3f")])
                          prog.add("dve", lambda e, ot=ot: e.tensor_add(
                              xo_t[ot][:], xo_t[ot][:], xn3f[:]),
                              reads=[B_("p7_xn3f")], writes=[B_(f"p7_xo{ot}")])
                          prog.add("sync", lambda e, ot=ot, tc=tc: e.dma_start(
                              out=x2t[:], in_=x2[ot * 128:(ot + 1) * 128,
                                                tc * 512:(tc + 1) * 512]),
                              reads=[B_("x2")], writes=[B_("p7_x2t")])
                          prog.add("dve", lambda e, ot=ot: e.tensor_add(
                              xo_t[ot][:], xo_t[ot][:], x2t[:]),
                              reads=[B_("p7_x2t")], writes=[B_(f"p7_xo{ot}")])
                          prog.add("sync", lambda e, ot=ot, tc=tc: e.dma_start(
                              out=x3[ot * 128:(ot + 1) * 128,
                                     tc * 512:(tc + 1) * 512],
                              in_=xo_t[ot][:]),
                              reads=[B_(f"p7_xo{ot}")], writes=[B_("x3")])
                          prog.add("pe", lambda e, ot=ot: e.matmul(
                              psA[:], onesC[:], xo_t[ot][:], start=(ot == 0),
                              stop=(ot == NCT - 1)),
                              reads=[B_(f"p7_xo{ot}"), B_("onesC")],
                              writes=[B_("p7_psA2")])
                          prog.add("act", lambda e, ot=ot: e.activation(
                              sq[:], xo_t[ot][:], AF.Square),
                              reads=[B_(f"p7_xo{ot}")], writes=[B_("p7_sq")])
                          prog.add("pe", lambda e, ot=ot: e.matmul(
                              psB[:], onesC[:], sq[:], start=(ot == 0),
                              stop=(ot == NCT - 1)),
                              reads=[B_("p7_sq"), B_("onesC")], writes=[B_("p7_psB2")])
                      ln_inline("ln4", xo_t, [B_(f"p7_xo{i}") for i in range(NCT)],
                                psA, psB, B_("p7_psA2"), B_("p7_psB2"), rows, xn4,
                                tc, psR, psM, B_("p7_psR2"), B_("p7_psM2"))
                  prog.barrier()
                  _emit_block(nc, prog, sems)

        if upto >= 7:
            bufs["f2_xnsrc"] = bufs.get("ln4_dst", Buf("f2_xnsrc"))
            bufs["f2_xressrc"] = bufs.get("x3", Buf("f2_xressrc"))
            ffn("f2", xn4, x3, xout, None)
            bufs["xout"] = bufs["f2_xo"]

        with ExitStack() as ph:
            prog.barrier()
            emit_taps()
            if upto < 7:
                tcopy = ph.enter_context(nc.sbuf_tensor("stub_t", [128, 2048], F32))
                for ct in range(NCT):
                    prog.add("sync", lambda e, ct=ct: e.dma_start(
                        out=tcopy[:], in_=x0[ct * 128:(ct + 1) * 128, :]),
                        writes=[B_("stub")])
                    prog.add("sync", lambda e, ct=ct: e.dma_start(
                        out=xout[ct * 128:(ct + 1) * 128, :], in_=tcopy[:]),
                        reads=[B_("stub")], writes=[B_("xout")])
            prog.barrier()
            _emit_block(nc, prog, sems, final=True)
        return nc



def _rope_tables():
    i = np.arange(32, dtype=np.float64)
    theta = 10000.0 ** (-2.0 * i / 64.0)
    t = np.arange(T, dtype=np.float64)
    tid = (np.arange(T) // 576) // 2
    ang = (t + tid)[:, None] * theta[None, :]
    c = np.cos(ang).T
    s = np.sin(ang).T
    Cfull = np.empty((128, T), np.float64)
    Sfull = np.empty((128, T), np.float64)
    for blk in range(4):
        r = slice(blk * 32, blk * 32 + 32)
        Cfull[r] = c
        Sfull[r] = s if (blk % 2 == 1) else -s
    return Cfull, Sfull


def _host_inputs(full, c):
    b, p = c // 2, c % 2
    f32 = np.float32
    d = {}
    xs = full["L0_attn_a_hidden_states"][b, p * NT:(p + 1) * NT, :]
    d["x0"] = np.ascontiguousarray(xs.T.astype(f32))
    hsl = slice(p * 1024, (p + 1) * 1024)
    d["wqT"] = np.ascontiguousarray(full["L0_attn_a_Wq"].T[:, hsl].astype(f32))
    d["wkT"] = np.ascontiguousarray(full["L0_attn_a_Wk"].T[:, hsl].astype(f32))
    d["wvT"] = np.ascontiguousarray(full["L0_attn_a_Wv"].T[:, hsl].astype(f32))
    d["woT"] = np.ascontiguousarray(full["L0_attn_a_Wo"].T.astype(f32))
    d["bq"] = np.ascontiguousarray(
        full["L0_attn_a_bq"][hsl].reshape(8, 128).T.astype(f32))
    d["bk"] = np.ascontiguousarray(
        full["L0_attn_a_bk"][hsl].reshape(8, 128).T.astype(f32))
    d["bv"] = np.ascontiguousarray(
        full["L0_attn_a_bv"][hsl].reshape(1, 1024).astype(f32))
    d["bo"] = np.ascontiguousarray(
        full["L0_attn_a_bo"].reshape(16, 128).T.astype(f32))
    d["pinT"] = np.ascontiguousarray(full["L0_attn_b_proj_in_w"].T.astype(f32))
    d["pinb"] = np.ascontiguousarray(
        full["L0_attn_b_proj_in_b"].reshape(16, 128).T.astype(f32))
    d["poutT"] = np.ascontiguousarray(full["L0_attn_b_proj_out_w"].T.astype(f32))
    d["poutb"] = np.ascontiguousarray(
        full["L0_attn_b_proj_out_b"].reshape(16, 128).T.astype(f32))
    d["BwT"] = np.ascontiguousarray(full["L0_attn_b_B_w"].T.astype(f32))
    d["CwT"] = np.ascontiguousarray(full["L0_attn_b_C_w"].T.astype(f32))
    d["Amat"] = np.ascontiguousarray(full["L0_attn_b_A"][:, p * 8:(p + 1) * 8].astype(f32))
    d["Dpar"] = np.ascontiguousarray(
        full["L0_attn_b_D_param"].reshape(16, 128).T.astype(f32))
    d["lut"] = np.ascontiguousarray(
        np.tile(full["L0_ffn_lut"].reshape(1, 16), (128, 1)).astype(f32))
    gi = np.zeros((D, IPAD), np.int32)
    gi[:, :INTER] = full["L0_ffn_gate_idx"].T
    ui = np.zeros((D, IPAD), np.int32)
    ui[:, :INTER] = full["L0_ffn_up_idx"].T
    d["gidxT"] = np.ascontiguousarray(gi[:, c * 768:(c + 1) * 768])
    d["uidxT"] = np.ascontiguousarray(ui[:, c * 768:(c + 1) * 768])
    d["didxT"] = np.ascontiguousarray(
        full["L0_ffn_down_idx"].T[:, c * 256:(c + 1) * 256].astype(np.int32))
    Cf, Sf = _rope_tables()
    d["ropeCq"] = np.ascontiguousarray((0.125 * Cf).astype(f32))
    d["ropeSq"] = np.ascontiguousarray((0.125 * Sf).astype(f32))
    d["ropeCk"] = np.ascontiguousarray(Cf.astype(f32))
    d["ropeSk"] = np.ascontiguousarray(Sf.astype(f32))
    m = np.zeros((128, 2048), np.float32)
    for o in range(4):
        k = np.arange(128)[:, None]
        q = np.arange(512)[None, :]
        m[:, o * 512:(o + 1) * 512] = (o * 128 + k <= q)
    d["dmask"] = m.astype(BF)
    E = np.zeros((128, 64), np.float32)
    for s_ in range(8):
        E[:, s_ * 8 + s_] = 1.0
    d["Eall"] = E.astype(BF)
    d["identb"] = np.eye(128).astype(BF)
    d["sel0"] = np.full((128, 1), 1.0 - p, np.float32)
    d["sel1"] = np.full((128, 1), float(p), np.float32)
    cva = np.zeros((1, 8), np.float32)
    cva[0, 0] = EPS
    cva[0, 1] = -0.5
    cva[0, 2] = 1.0 / D
    d["cvals"] = np.ascontiguousarray(np.tile(cva, (128, 1)))
    return d


def kernel(**inputs):
    nc = build()
    in_maps = [_host_inputs(inputs, c) for c in range(8)]
    res = run_bass_kernel_spmd(nc, in_maps, core_ids=list(range(8)))
    out = np.empty((B, T, D), np.float32)
    for c in range(8):
        b, p = c // 2, c % 2
        out[b, p * NT:(p + 1) * NT, :] = np.asarray(
            res.results[c]["xout"], dtype=np.float32).T
    return out


# version 20
# speedup vs baseline: 1.1287x; 1.0194x over previous
import math
from contextlib import ExitStack

import numpy as np
import ml_dtypes

import concourse.bass as bass
import concourse.mybir as mybir
from concourse.bass_utils import run_bass_kernel_spmd

F32 = mybir.dt.float32
BF16 = mybir.dt.bfloat16
I32 = mybir.dt.int32
AF = mybir.ActivationFunctionType
ALU = mybir.AluOpType
BF = ml_dtypes.bfloat16

B, T, D = 4, 4096, 2048
H, DH = 32, 64
INTER = 5504
NIT = 43
IPAD = 6144
S = 16
NT = 2048
NCT = 16
EPS = 1e-5

ENGINES = ("sync", "pe", "act", "dve", "pool")


class Buf:
    __slots__ = ("name", "w", "rs")

    def __init__(self, name):
        self.name = name
        self.w = {}
        self.rs = []


N_DSEM = 40

DRAM_BUFS = None


class Prog:

    def __init__(self, is_dram):
        self.ops = {e: [] for e in ENGINES}
        self.cum = {e: 0 for e in ENGINES}
        self.emit_pos = {e: 0 for e in ENGINES}
        self.know = {e: {} for e in ENGINES}
        self.is_dram = is_dram
        self.dcount = {}
        self.dkey2sem = {}
        self.dsem_free = [f"d{i}" for i in range(N_DSEM)]
        self.dsem_inuse = []
        self.nbar = 0

    def _dsem_for(self, key):
        if key not in self.dkey2sem:
            if not self.dsem_free:
                raise RuntimeError("out of dma sems; add a barrier")
            nm = self.dsem_free.pop(0)
            self.dsem_inuse.append(nm)
            self.dkey2sem[key] = nm
        return self.dkey2sem[key]

    def _semkey(self, reads, writes):
        for b in writes:
            if not self.is_dram(b.name):
                return b.name
        for b in reads:
            if not self.is_dram(b.name):
                return b.name
        return "misc"

    def add(self, eng, fn, reads=(), writes=(), inc=None, deps=()):
        is_dma = inc is not None and inc == 16 and eng in ("sync", "pool")
        is_cc = inc == 1 and eng == "pool"
        if inc is None:
            inc = 16 if eng == "sync" else 1
            is_dma = eng == "sync"
        dd = {}

        def _merge(d):
            if d is not None:
                sn, v = d
                dd[sn] = max(dd.get(sn, 0), v)

        for b in reads:
            for it in b.w.items():
                _merge(it)
            if "_ps" in b.name:
                for r in b.rs:
                    if r[0] != eng:
                        _merge(r)
        for b in writes:
            for it in b.w.items():
                _merge(it)
            for r in b.rs:
                _merge(r)
        for d in deps:
            _merge(d)
        if self.nbar:
            dd["bar"] = max(dd.get("bar", 0), self.nbar)

        if is_dma or is_cc:
            key = self._semkey(reads, writes)
            sn = self._dsem_for(key)
            self.dcount[sn] = self.dcount.get(sn, 0) + inc
            tok = (sn, self.dcount[sn])
            self.ops[eng].append((fn, (sn, inc), None, dd))
        else:
            self.cum[eng] += inc
            tok = (eng, self.cum[eng])
            self.ops[eng].append((fn, (eng, inc), None, dd))
        for b in reads:
            if is_dma or is_cc:
                b.rs.append((tok[0], self.dcount[tok[0]]))
            else:
                b.rs.append(tok)
        for b in writes:
            if is_dma or is_cc:
                b.w[tok[0]] = self.dcount[tok[0]]
            else:
                b.w = {tok[0]: tok[1]}
            b.rs = []
        return tok

    def barrier(self):
        self.nbar += 1
        for qeng in ("sync", "pool"):
            dd = {sn: self.dcount[sn] for sn in self.dsem_inuse
                  if self.dcount.get(sn, 0)}
            self.ops[qeng].append(
                (lambda e: e.nop(), ("bar", 1 if qeng == "sync" else 0), None, dd))
        self.bar_cums = dict(self.cum)
        self._bar_extra = {e: self.cum[e] for e in ("pe", "act", "dve", "pool")
                           if self.cum[e]}
        self.ops["sync"][-1][3].update(
            {e: v for e, v in self._bar_extra.items()})
        self.ops["pool"][-1][3].update(
            {e: v for e, v in self._bar_extra.items()
             if e != "pool"})
        self.dkey2sem = {}
        self.dsem_free = [f"d{i}" for i in range(N_DSEM)]
        self.dsem_inuse = []

    def emit_phase(self, ename, eng, sems):
        know = self.know[ename]
        ops = self.ops[ename]
        pos = self.emit_pos[ename]
        for fn, (sn, inc), _, dd in ops[pos:]:
            for dn, v in sorted(dd.items()):
                if dn == "pe" and ename == "pe":
                    continue
                if v > know.get(dn, 0):
                    eng.wait_ge(sems[dn], v)
                    know[dn] = v
            inst = fn(eng)
            if inst is not None and inc:
                inst.then_inc(sems[sn], inc)
        self.emit_pos[ename] = len(ops)


def _emit_block(nc, prog, sems, final=False):
    with nc.Block() as blk:
        blk.sync(lambda e: (prog.emit_phase("sync", e, sems),
                            final and _final_waits(e, prog, sems)))
        blk.tensor(lambda e: prog.emit_phase("pe", e, sems))
        blk.scalar(lambda e: prog.emit_phase("act", e, sems))
        blk.vector(lambda e: prog.emit_phase("dve", e, sems))
        blk.gpsimd(lambda e: prog.emit_phase("pool", e, sems))


def _final_waits(eng, prog, sems):
    for e2 in ENGINES:
        if prog.cum[e2]:
            eng.wait_ge(sems[e2], prog.cum[e2])
    for sn, v in prog.dcount.items():
        if v:
            eng.wait_ge(sems[sn], v)
    return True


def build(taps=frozenset(), upto=99, p7sub=4):
    nc = bass.Bass("TRN2", target_bir_lowering=False)
    taps = set(taps)

    def din(name, shape, dtype=F32):
        return nc.dram_tensor(name, shape, dtype, kind="ExternalInput").ap()

    tapreqs = []

    def dint(name, shape, dtype):
        if name in taps:
            iap = nc.dram_tensor(name + "_i", shape, dtype, kind="Internal").ap()
            oap = nc.dram_tensor(name, shape, dtype, kind="ExternalOutput").ap()
            tapreqs.append((name, iap, oap, shape, dtype))
            return iap
        return nc.dram_tensor(name, shape, dtype, kind="Internal").ap()

    x0 = din("x0", [D, NT])
    wqT = din("wqT", [D, 1024])
    wkT = din("wkT", [D, 1024])
    wvT = din("wvT", [D, 1024])
    woT = din("woT", [D, D])
    bq = din("bq", [128, 8])
    bk = din("bk", [128, 8])
    bv = din("bv", [1, 1024])
    bo = din("bo", [128, 16])
    pinT = din("pinT", [D, D])
    pinb = din("pinb", [128, 16])
    poutT = din("poutT", [D, D])
    poutb = din("poutb", [128, 16])
    BwT = din("BwT", [D, S])
    CwT = din("CwT", [S, D])
    Amat = din("Amat", [D, 8])
    Dpar = din("Dpar", [128, 16])
    lut = din("lut", [128, 16])
    gidxT = din("gidxT", [D, 768], I32)
    uidxT = din("uidxT", [D, 768], I32)
    didxT = din("didxT", [INTER, 256], I32)
    ropeCq = din("ropeCq", [128, T])
    ropeSq = din("ropeSq", [128, T])
    ropeCk = din("ropeCk", [128, T])
    ropeSk = din("ropeSk", [128, T])
    dmask = din("dmask", [128, 2048], BF16)
    Eall = din("Eall", [128, 64], BF16)
    cvals = din("cvals", [128, 8])
    identb = din("identb", [128, 128], BF16)
    sel0 = din("sel0", [128, 1])
    sel1 = din("sel1", [128, 1])

    GN = 16 * 6 * 128 * 128
    DN = 43 * 2 * 128 * 128
    WSH_N = 2 * GN + DN
    xn1_my = dint("xn1_my", [D, NT], BF16)
    xn1_g = dint("xn1_g", [2 * D, NT], BF16)
    qrot = dint("qrot", [1024, T], BF16)
    krot = dint("krot", [1024, T], BF16)
    vt = dint("vt", [T, 1024], BF16)
    attn_my = dint("attn_my", [1024, T], BF16)
    attn_g = dint("attn_g", [2048, T], BF16)
    wsh = dint("wsh", [1, WSH_N], BF16)
    wg8 = dint("wg8", [8, WSH_N], BF16)
    x1 = dint("x1", [D, NT], F32)
    xn2 = dint("xn2", [D, NT], BF16)
    x2 = dint("x2", [D, NT], F32)
    xn3 = dint("xn3", [D, NT], BF16)
    x3 = dint("x3", [D, NT], F32)
    xn4 = dint("xn4", [D, NT], BF16)
    xp_my = dint("xp_my", [D, NT], BF16)
    xp_g = dint("xp_g", [2 * D, NT], BF16)
    dT_my = dint("dT_my", [S, NT], BF16)
    dT_g = dint("dT_g", [2 * S, NT], BF16)
    hbuf = dint("hbuf", [INTER, NT], BF16)
    wo_bf = dint("wo_bf", [D, D], BF16)
    pin_bf = dint("pin_bf", [D, D], BF16)
    pout_bf = dint("pout_bf", [D, D], BF16)
    G_my = dint("G_my", [8, T], BF16)
    G_g = dint("G_g", [16, T], BF16)
    xout = nc.dram_tensor("xout", [D, NT], F32, kind="ExternalOutput").ap()
    import os as _osd
    DBG7 = _osd.environ.get("DBG7", "0") == "1"
    if DBG7:
        dbg_pout = nc.dram_tensor("dbg_pout", [128, 32768], BF16,
                                  kind="ExternalOutput").ap()
        dbg_y = nc.dram_tensor("dbg_y", [128, 32768], BF16,
                               kind="ExternalOutput").ap()

    _DRAM_NAMES = {"ln1_dst", "xn1_g", "qkstage", "vtstage", "attn_my",
                   "attn_g", "wsh", "wg8", "x1", "x2", "x3", "xout", "hbuf",
                   "xp_my", "xp_g", "dT_my", "dT_g", "G_my", "G_g",
                   "xn3_dst", "stub", "dbgp", "dbgy", "misc", "wo_bf", "pin_bf", "pout_bf", "wobf", "pinbf", "poutbf"}

    def _is_dram(name):
        return (name in _DRAM_NAMES or name.endswith("_tapout")
                or name.endswith("_dst") or name.endswith("_xo")
                or name.endswith("_xnsrc") or name.endswith("_xressrc"))

    prog = Prog(_is_dram)
    bufs = {}

    def B_(name):
        if name not in bufs:
            bufs[name] = Buf(name)
        return bufs[name]

    with ExitStack() as outer:
        sems = {e: outer.enter_context(nc.semaphore(f"sem_{e}")) for e in ENGINES}
        sems["bar"] = outer.enter_context(nc.semaphore("sem_bar"))
        for i in range(N_DSEM):
            sems[f"d{i}"] = outer.enter_context(nc.semaphore(f"dsem_{i}"))

        sb = lambda n, s, d: outer.enter_context(nc.sbuf_tensor(n, s, d))
        cv = sb("cv", [128, 8], F32)
        lut_s = sb("lut_s", [128, 16], F32)
        E_s = sb("E_s", [128, 64], BF16)
        ones1 = sb("ones1", [1, 128], F32)
        ones1b = sb("ones1b", [1, 128], BF16)
        onesC = sb("onesC", [128, 1], F32)
        onesCb = sb("onesCb", [128, 8], BF16)
        sigA = sb("sigA", [128, 128], F32)
        sigAb = sb("sigAb", [128, 128], BF16)
        BwT_s = sb("BwT_s", [128, 256], BF16)
        CwT_s = sb("CwT_s", [16, D], BF16)
        D_s = sb("D_s", [128, 16], F32)
        bq_s = sb("bq_s", [128, 8], F32)
        bk_s = sb("bk_s", [128, 8], F32)
        bo_s = sb("bo_s", [128, 16], F32)
        pinb_s = sb("pinb_s", [128, 16], F32)
        poutb_s = sb("poutb_s", [128, 16], F32)
        bvb = sb("bvb", [128, 1024], F32)
        mask_s = sb("mask_s", [128, 2048], BF16)
        identb_s = sb("identb_s", [128, 128], BF16)
        bidx = {en: sb(f"p4_bidx_{en}", [128, 768], I32) for en in ("dve",)}
        bidf = {en: sb(f"p4_bidf_{en}", [128, 768], BF16) for en in ("dve",)}
        bcon = {en: sb(f"p4_bcon_{en}", [128, 768], BF16) for en in ("dve",)}
        bacc = {en: sb(f"p4_bacc_{en}", [128, 768], BF16) for en in ("dve",)}
        sel0c = sb("sel0c", [128, 1], F32)
        sel1c = sb("sel1c", [128, 1], F32)

        with ExitStack() as ph:
            psb = lambda n, s, d: ph.enter_context(nc.sbuf_tensor(n, s, d))
            pps = lambda n, s: ph.enter_context(nc.psum_tensor(n, s, F32))
            tmpA = psb("p0_tmpA", [128, 256], F32)
            tmpB = psb("p0_tmpB", [16, D], F32)
            bvrow = psb("p0_bvrow", [1, 1024], F32)
            psb0 = pps("p0_ps", [128, 512])

            for nm, dst, src in [("cv", cv, cvals), ("lut_s", lut_s, lut),
                                 ("D_s", D_s, Dpar), ("bq_s", bq_s, bq),
                                 ("bk_s", bk_s, bk), ("bo_s", bo_s, bo),
                                 ("pinb_s", pinb_s, pinb),
                                 ("poutb_s", poutb_s, poutb)]:
                prog.add("sync", lambda e, d=dst, s=src: e.dma_start(out=d[:], in_=s),
                         writes=[B_("c_" + nm)])
            prog.add("sync", lambda e: e.dma_start(out=E_s[:], in_=Eall),
                     writes=[B_("E_s")])
            prog.add("sync", lambda e: e.dma_start(out=identb_s[:], in_=identb),
                     writes=[B_("c_identb")])
            prog.add("sync", lambda e: e.dma_start(out=sel0c[:], in_=sel0),
                     writes=[B_("c_sel")])
            prog.add("sync", lambda e: e.dma_start(out=sel1c[:], in_=sel1),
                     writes=[B_("c_sel")])
            prog.add("sync", lambda e: e.dma_start(out=mask_s[:], in_=dmask),
                     writes=[B_("mask_s")])
            prog.add("sync", lambda e: e.dma_start(out=bvrow[:], in_=bv),
                     writes=[B_("bvrow")])
            prog.add("dve", lambda e: e.memset(ones1[:], 1.0), writes=[B_("ones1")])
            prog.add("dve", lambda e: e.memset(ones1b[:], 1.0), writes=[B_("ones1b")])
            prog.add("dve", lambda e: e.memset(onesC[:], 1.0), writes=[B_("onesC")])
            prog.add("dve", lambda e: e.memset(onesCb[:], 1.0), writes=[B_("onesCb")])
            prog.add("sync", lambda e: e.dma_start(
                out=tmpA[:, 0:128].rearrange("p (ct s) -> p ct s", ct=16),
                in_=Amat.rearrange("(ct p) s -> p ct s", p=128)),
                writes=[B_("tmpA")])
            prog.add("act", lambda e: e.activation(sigA[:], tmpA[:, 0:128],
                                                   AF.Sigmoid),
                     reads=[B_("tmpA")], writes=[B_("sigA")])
            prog.add("dve", lambda e: e.tensor_copy(sigAb[:], sigA[:]),
                     reads=[B_("sigA")], writes=[B_("sigAb")])
            prog.add("sync", lambda e: e.dma_start(
                out=tmpA[:].rearrange("p (ct s) -> p ct s", ct=16),
                in_=BwT.rearrange("(ct p) s -> p ct s", p=128)),
                writes=[B_("tmpA")])
            prog.add("dve", lambda e: e.tensor_copy(BwT_s[:], tmpA[:]),
                     reads=[B_("tmpA")], writes=[B_("BwT_s")])
            prog.add("sync", lambda e: e.dma_start(out=tmpB[:], in_=CwT),
                     writes=[B_("tmpB")])
            prog.add("dve", lambda e: e.tensor_copy(CwT_s[:], tmpB[:]),
                     reads=[B_("tmpB")], writes=[B_("CwT_s")])
            for h in range(2):
                prog.add("pe", lambda e, h=h: e.matmul(
                    psb0[:, 0:512], ones1[:], bvrow[0:1, h * 512:(h + 1) * 512],
                    start=True, stop=True),
                    reads=[B_("ones1"), B_("bvrow")], writes=[B_("p0_ps")])
                prog.add("dve", lambda e, h=h: e.tensor_copy(
                    bvb[:, h * 512:(h + 1) * 512], psb0[:, 0:512]),
                    reads=[B_("p0_ps")], writes=[B_("bvb")])
            prog.barrier()
            _emit_block(nc, prog, sems)

        def ln_pass(tag, src_ap, dst_ap, extra_dst_my=None):
            with ExitStack() as ph:
                psb = lambda n, s, d: ph.enter_context(nc.sbuf_tensor(n, s, d))
                pps = lambda n, s: ph.enter_context(nc.psum_tensor(n, s, F32))
                xt = [[psb(f"{tag}_x{i}_{pr}", [128, 512], F32)
                       for i in range(NCT)] for pr in (0, 1)]
                sq = [psb(f"{tag}_sq{pr}", [128, 512], F32) for pr in (0, 1)]
                rowm = [psb(f"{tag}_m{pr}", [1, 512], F32) for pr in (0, 1)]
                rowq = [psb(f"{tag}_q{pr}", [1, 512], F32) for pr in (0, 1)]
                rowr = [psb(f"{tag}_r{pr}", [1, 512], F32) for pr in (0, 1)]
                rowmb = [psb(f"{tag}_mb{pr}", [1, 512], F32) for pr in (0, 1)]
                xnt = [psb(f"{tag}_xn{pr}", [128, 512], BF16) for pr in (0, 1)]
                t1 = [psb(f"{tag}_t1{pr}", [128, 512], F32) for pr in (0, 1)]
                psA = [pps(f"{tag}_psA{pr}", [1, 512]) for pr in (0, 1)]
                psB = [pps(f"{tag}_psB{pr}", [1, 512]) for pr in (0, 1)]
                psR = [pps(f"{tag}_psR{pr}", [128, 512]) for pr in (0, 1)]
                psM = [pps(f"{tag}_psM{pr}", [128, 512]) for pr in (0, 1)]
                for tc in range(4):
                    pr = tc % 2
                    bx = [B_(f"{tag}_x{i}_{pr}") for i in range(NCT)]
                    for ct in range(NCT):
                        prog.add("sync", lambda e, ct=ct, tc=tc, pr=pr: e.dma_start(
                            out=xt[pr][ct][:],
                            in_=src_ap[ct * 128:(ct + 1) * 128,
                                       tc * 512:(tc + 1) * 512]),
                            writes=[bx[ct]])
                        prog.add("pe", lambda e, ct=ct, pr=pr: e.matmul(
                            psA[pr][:], onesC[:], xt[pr][ct][:], start=(ct == 0),
                            stop=(ct == NCT - 1)),
                            reads=[bx[ct], B_("onesC")], writes=[B_(f"{tag}_psA{pr}")])
                        prog.add("act", lambda e, ct=ct, pr=pr: e.activation(
                            sq[pr][:], xt[pr][ct][:], AF.Square),
                            reads=[bx[ct]], writes=[B_(f"{tag}_sq{pr}")])
                        prog.add("pe", lambda e, ct=ct, pr=pr: e.matmul(
                            psB[pr][:], onesC[:], sq[pr][:], start=(ct == 0),
                            stop=(ct == NCT - 1)),
                            reads=[B_(f"{tag}_sq{pr}"), B_("onesC")],
                            writes=[B_(f"{tag}_psB{pr}")])
                    prog.add("dve", lambda e, pr=pr: e.tensor_scalar_mul(
                        rowm[pr][:], psA[pr][:], 1.0 / D),
                        reads=[B_(f"{tag}_psA{pr}")], writes=[B_(f"{tag}_m{pr}")])
                    prog.add("dve", lambda e, pr=pr: e.tensor_scalar_mul(
                        rowq[pr][:], psB[pr][:], 1.0 / D),
                        reads=[B_(f"{tag}_psB{pr}")], writes=[B_(f"{tag}_q{pr}")])
                    prog.add("dve", lambda e, pr=pr: e.scalar_tensor_tensor(
                        rowr[pr][:], rowm[pr][:], 1.0, rowm[pr][:],
                        ALU.mult, ALU.mult),
                        reads=[B_(f"{tag}_m{pr}")], writes=[B_(f"{tag}_r{pr}")])
                    prog.add("dve", lambda e, pr=pr: e.tensor_sub(
                        rowq[pr][:], rowq[pr][:], rowr[pr][:]),
                        reads=[B_(f"{tag}_q{pr}"), B_(f"{tag}_r{pr}")],
                        writes=[B_(f"{tag}_q{pr}")])
                    prog.add("act", lambda e, pr=pr: e.activation(
                        rowr[pr][:], rowq[pr][:], AF.Ln, bias=cv[0:1, 0:1]),
                        reads=[B_(f"{tag}_q{pr}"), B_("c_cv")],
                        writes=[B_(f"{tag}_r{pr}")])
                    prog.add("act", lambda e, pr=pr: e.activation(
                        rowr[pr][:], rowr[pr][:], AF.Exp, scale=cv[0:1, 1:2]),
                        reads=[B_(f"{tag}_r{pr}")], writes=[B_(f"{tag}_r{pr}")])
                    prog.add("dve", lambda e, pr=pr: e.scalar_tensor_tensor(
                        rowmb[pr][:], rowm[pr][:], -1.0, rowr[pr][:],
                        ALU.mult, ALU.mult),
                        reads=[B_(f"{tag}_m{pr}"), B_(f"{tag}_r{pr}")],
                        writes=[B_(f"{tag}_mb{pr}")])
                    prog.add("pe", lambda e, pr=pr: e.matmul(
                        psR[pr][:], ones1[:], rowr[pr][:], start=True, stop=True),
                        reads=[B_(f"{tag}_r{pr}"), B_("ones1")],
                        writes=[B_(f"{tag}_psR{pr}")])
                    prog.add("pe", lambda e, pr=pr: e.matmul(
                        psM[pr][:], ones1[:], rowmb[pr][:], start=True, stop=True),
                        reads=[B_(f"{tag}_mb{pr}"), B_("ones1")],
                        writes=[B_(f"{tag}_psM{pr}")])
                    for ct in range(NCT):
                        prog.add("dve", lambda e, ct=ct, pr=pr: e.tensor_mul(
                            t1[pr][:], xt[pr][ct][:], psR[pr][:]),
                            reads=[bx[ct], B_(f"{tag}_psR{pr}")],
                            writes=[B_(f"{tag}_t1{pr}")])
                        prog.add("dve", lambda e, pr=pr: e.tensor_add(
                            xnt[pr][:], t1[pr][:], psM[pr][:]),
                            reads=[B_(f"{tag}_t1{pr}"), B_(f"{tag}_psM{pr}")],
                            writes=[B_(f"{tag}_xn{pr}")])
                        prog.add("sync", lambda e, ct=ct, tc=tc, pr=pr: e.dma_start(
                            out=dst_ap[ct * 128:(ct + 1) * 128,
                                       tc * 512:(tc + 1) * 512],
                            in_=xnt[pr][:]),
                            reads=[B_(f"{tag}_xn{pr}")], writes=[B_(tag + "_dst")])
                prog.barrier()
                _emit_block(nc, prog, sems)

        if upto >= 1:
            ln_pass("ln1", x0, xn1_my)

        def emit_taps():
            for name, iap, oap, shape, dtype in tapreqs:
                prog.add("pool", lambda e, i=iap, o=oap: e.dma_start(out=o, in_=i),
                         reads=[B_(name.replace("_i", ""))], deps=(),
                         writes=[B_(name + "_tapout")], inc=16)

        G_OFF = 0
        U_OFF = D * 768
        D_OFF = 2 * D * 768
        btiles = []
        for ct in range(NCT):
            btiles.append((gidxT, ct * 128, 768, G_OFF + ct * 128 * 768))
        for ct in range(NCT):
            btiles.append((uidxT, ct * 128, 768, U_OFF + ct * 128 * 768))
        for it in range(NIT):
            btiles.append((didxT, it * 128, 256, D_OFF + it * 128 * 256))
        build_thunks = []

        def emit_btile(src, r0, w, off):
            en2 = "dve"
            build_thunks.append(lambda src=src, r0=r0, w=w, en2=en2: prog.add(
                "sync", lambda e: e.dma_start(
                    out=bidx[en2][:, 0:w], in_=src[r0:r0 + 128, :]),
                writes=[B_(f"p4_bidx_{en2}")], inc=16))
            build_thunks.append(lambda w=w, en2=en2: prog.add(
                en2, lambda e: e.tensor_copy(
                    bidf[en2][:, 0:w], bidx[en2][:, 0:w]),
                reads=[B_(f"p4_bidx_{en2}")],
                writes=[B_(f"p4_bidf_{en2}")]))
            for j in range(16):
                if j == 0:
                    build_thunks.append(lambda w=w, en2=en2: prog.add(
                        en2, lambda e: e.tensor_scalar(
                            bacc[en2][:, 0:w], bidf[en2][:, 0:w], 0.0,
                            lut_s[:, 0:1], ALU.is_equal, ALU.mult),
                        reads=[B_(f"p4_bidf_{en2}"), B_("c_lut_s")],
                        writes=[B_(f"p4_bacc_{en2}")]))
                else:
                    build_thunks.append(lambda w=w, j=j, en2=en2: prog.add(
                        en2, lambda e: e.tensor_scalar(
                            bcon[en2][:, 0:w], bidf[en2][:, 0:w], float(j),
                            lut_s[:, j:j + 1], ALU.is_equal, ALU.mult),
                        reads=[B_(f"p4_bidf_{en2}"), B_("c_lut_s")],
                        writes=[B_(f"p4_bcon_{en2}")]))
                    build_thunks.append(lambda w=w, en2=en2: prog.add(
                        en2, lambda e: e.tensor_add(
                            bacc[en2][:, 0:w], bacc[en2][:, 0:w],
                            bcon[en2][:, 0:w]),
                        reads=[B_(f"p4_bcon_{en2}")],
                        writes=[B_(f"p4_bacc_{en2}")]))
            nt_r = 16 if w == 768 else NIT
            region = G_OFF if off < U_OFF else (U_OFF if off < D_OFF else D_OFF)
            ti = (off - region) // (128 * w)
            build_thunks.append(lambda region=region, ti=ti, w=w,
                                nt_r=nt_r, en2=en2: prog.add(
                "sync", lambda e: e.dma_start(
                    out=wsh[0, region:region + 128 * nt_r * w]
                    .rearrange("(p t i) -> p t i", p=128, t=nt_r)[:, ti, :],
                    in_=bacc[en2][:, 0:w]),
                reads=[B_(f"p4_bacc_{en2}")], writes=[B_("wsh")], inc=16))

        for bt in btiles:
            emit_btile(*bt)
        bt_i = [0]

        def drain_build(k):
            while k > 0 and bt_i[0] < len(build_thunks):
                build_thunks[bt_i[0]]()
                bt_i[0] += 1
                k -= 1

        if upto >= 2:
            with ExitStack() as ph:
                psb = lambda n, s, d: ph.enter_context(nc.sbuf_tensor(n, s, d))
                pps = lambda n, s: ph.enter_context(nc.psum_tensor(n, s, F32))
                wq_s = psb("p2_wq", [128, 16 * 1024], BF16)
                wk_s = psb("p2_wk", [128, 16 * 1024], BF16)
                wv_s = psb("p2_wv", [128, 16 * 1024], BF16)
                wtmp = psb("p2_wtmp", [128, 1024], F32)
                cvt32 = [psb(f"p2_cvt32{i}", [128, 2048], F32) for i in (0, 1)]
                cvt16 = [psb(f"p2_cvt16{i}", [128, 2048], BF16) for i in (0, 1)]
                prog.add("pool", lambda e: e.collective_compute(
                    "AllGather", ALU.bypass, [[0, 1], [2, 3], [4, 5], [6, 7]],
                    ins=[xn1_my.opt()], outs=[xn1_g.opt()]),
                    reads=[B_("ln1_dst")], writes=[B_("xn1_g")], inc=1)
                xch = [psb(f"p2_xch{i}", [128, 16 * 512], BF16) for i in (0, 1)]
                Ctq = psb("p2_Ctq", [128, 512], F32)
                Stq = psb("p2_Stq", [128, 512], F32)
                Ctk = psb("p2_Ctk", [128, 512], F32)
                Stk = psb("p2_Stk", [128, 512], F32)
                qf = [psb(f"p2_qf{i}", [128, 512], F32) for i in (0, 1)]
                qsw = [psb(f"p2_qsw{i}", [128, 512], F32) for i in (0, 1)]
                qt1 = [psb(f"p2_qt1{i}", [128, 512], F32) for i in (0, 1)]
                qt2 = [psb(f"p2_qt2{i}", [128, 512], F32) for i in (0, 1)]
                qbf = [psb(f"p2_qbf{i}", [128, 512], BF16) for i in (0, 1)]
                vbf = psb("p2_vbf", [128, 512], BF16)
                psq = pps("p2_psq", [128, 512])
                psk = pps("p2_psk", [128, 512])
                psv = [pps(f"p2_psv{i}", [128, 512]) for i in (0, 1)]
                for wsrc, wdst_d, nm in ((woT, wo_bf, "wobf"),
                                         (pinT, pin_bf, "pinbf"),
                                         (poutT, pout_bf, "poutbf")):
                    for ct in range(NCT):
                        prog.add("sync", lambda e, srcw=wsrc, ct=ct: e.dma_start(
                            out=cvt32[ct % 2][:],
                            in_=srcw[ct * 128:(ct + 1) * 128, :]),
                            writes=[B_(f"p2_cvt32{ct % 2}")])
                        prog.add("dve", lambda e, ct=ct: e.tensor_copy(
                            cvt16[ct % 2][:], cvt32[ct % 2][:]),
                            reads=[B_(f"p2_cvt32{ct % 2}")],
                            writes=[B_(f"p2_cvt16{ct % 2}")])
                        prog.add("sync", lambda e, d=wdst_d, ct=ct: e.dma_start(
                            out=d[ct * 128:(ct + 1) * 128, :],
                            in_=cvt16[ct % 2][:]),
                            reads=[B_(f"p2_cvt16{ct % 2}")], writes=[B_(nm)])
                for wsrc, wdst, nm in ((wqT, wq_s, "wq"), (wkT, wk_s, "wk"),
                                       (wvT, wv_s, "wv")):
                    for ct in range(NCT):
                        prog.add("sync", lambda e, s=wsrc, ct=ct: e.dma_start(
                            out=wtmp[:], in_=s[ct * 128:(ct + 1) * 128, :]),
                            writes=[B_("p2_wtmp")])
                        prog.add("dve", lambda e, d=wdst, ct=ct: e.tensor_copy(
                            d[:, ct * 1024:(ct + 1) * 1024], wtmp[:]),
                            reads=[B_("p2_wtmp")], writes=[B_("p2_" + nm)])
                for tc in range(8):
                    drain_build(160)
                    slot, tcl = tc // 4, tc % 4
                    xb = tc % 2
                    for ct in range(NCT):
                        prog.add("sync", lambda e, ct=ct, slot=slot, tcl=tcl,
                                 xb=xb: e.dma_start(
                                     out=xch[xb][:, ct * 512:(ct + 1) * 512],
                                     in_=xn1_g[slot * D + ct * 128:
                                               slot * D + (ct + 1) * 128,
                                               tcl * 512:(tcl + 1) * 512]),
                                 reads=[B_("xn1_g")], writes=[B_(f"p2_xch{xb}")])
                    for tab, src in ((Ctq, ropeCq), (Stq, ropeSq),
                                     (Ctk, ropeCk), (Stk, ropeSk)):
                        prog.add("sync", lambda e, t=tab, s=src, tc=tc: e.dma_start(
                            out=t[:], in_=s[:, tc * 512:(tc + 1) * 512]),
                            writes=[B_("p2_tab")])
                    for ot in range(8):
                        for si, (wbuf, wnm, bcol, Ct, St, dst, psx, psnm) in \
                                enumerate((
                                (wq_s, "wq", bq_s, Ctq, Stq, qrot, psq, "p2_psq"),
                                (wk_s, "wk", bk_s, Ctk, Stk, krot, psk, "p2_psk"))):
                            for ct in range(NCT):
                                prog.add("pe", lambda e, w=wbuf, ct=ct, ot=ot,
                                         psx=psx, xb=xb: e.matmul(
                                             psx[:],
                                             w[:, ct * 1024 + ot * 128:
                                               ct * 1024 + (ot + 1) * 128],
                                             xch[xb][:, ct * 512:(ct + 1) * 512],
                                             start=(ct == 0),
                                             stop=(ct == NCT - 1)),
                                         reads=[B_("p2_" + wnm), B_(f"p2_xch{xb}")],
                                         writes=[B_(psnm)])
                            prog.add("act", lambda e, bc=bcol, ot=ot, psx=psx,
                                     si=si, psnm=psnm: e.activation(
                                         qf[si][:], psx[:], AF.Identity,
                                         bias=bc[:, ot:ot + 1]),
                                reads=[B_(psnm), B_("c_bq_s"), B_("c_bk_s")],
                                writes=[B_(f"p2_qf{si}")])
                            qeng = "pool" if si == 0 else "sync"
                            for (d0, d1) in ((0, 32), (32, 0), (64, 96), (96, 64)):
                                prog.add(qeng, lambda e, d0=d0, d1=d1, si=si:
                                         e.dma_start(
                                             out=qsw[si][d0:d0 + 32, :],
                                             in_=qf[si][d1:d1 + 32, :]),
                                         reads=[B_(f"p2_qf{si}")],
                                         writes=[B_(f"p2_qsw{si}")], inc=16)
                            prog.add("dve", lambda e, Ct=Ct, si=si: e.tensor_mul(
                                qt1[si][:], qf[si][:], Ct[:]),
                                reads=[B_(f"p2_qf{si}"), B_("p2_tab")],
                                writes=[B_(f"p2_qt1{si}")])
                            prog.add("dve", lambda e, St=St, si=si: e.tensor_mul(
                                qt2[si][:], qsw[si][:], St[:]),
                                reads=[B_(f"p2_qsw{si}"), B_("p2_tab")],
                                writes=[B_(f"p2_qt2{si}")])
                            prog.add("dve", lambda e, si=si: e.tensor_add(
                                qbf[si][:], qt1[si][:], qt2[si][:]),
                                reads=[B_(f"p2_qt1{si}"), B_(f"p2_qt2{si}")],
                                writes=[B_(f"p2_qbf{si}")])
                            prog.add("sync", lambda e, d=dst, ot=ot, tc=tc, si=si:
                                     e.dma_start(
                                         out=d[ot * 128:(ot + 1) * 128,
                                               tc * 512:(tc + 1) * 512],
                                         in_=qbf[si][:]),
                                     reads=[B_(f"p2_qbf{si}")], writes=[B_("qkstage")])
                    for sub in range(4):
                        kb = tc * 4 + sub
                        for half in range(2):
                            vp = (sub * 2 + half) % 2
                            for ct in range(NCT):
                                prog.add("pe", lambda e, ct=ct, sub=sub, half=half,
                                         vp=vp, xb=xb: e.matmul(
                                             psv[vp][:],
                                             xch[xb][:, ct * 512 + sub * 128:
                                                 ct * 512 + (sub + 1) * 128],
                                             wv_s[:, ct * 1024 + half * 512:
                                                  ct * 1024 + (half + 1) * 512],
                                             start=(ct == 0), stop=(ct == NCT - 1)),
                                         reads=[B_("p2_wv"), B_(f"p2_xch{xb}")],
                                         writes=[B_(f"p2_psv{vp}")])
                            prog.add("dve", lambda e, half=half, vp=vp: e.tensor_add(
                                vbf[:], psv[vp][:],
                                bvb[:, half * 512:(half + 1) * 512]),
                                reads=[B_(f"p2_psv{vp}"), B_("bvb")],
                                writes=[B_("p2_vbf")])
                            prog.add("sync", lambda e, kb=kb, half=half:
                                     e.dma_start(
                                         out=vt[kb * 128:(kb + 1) * 128,
                                                half * 512:(half + 1) * 512],
                                         in_=vbf[:]),
                                     reads=[B_("p2_vbf")], writes=[B_("vtstage")])
                prog.barrier()
                _emit_block(nc, prog, sems)

        if upto >= 3:
            with ExitStack() as ph:
                psb = lambda n, s, d: ph.enter_context(nc.sbuf_tensor(n, s, d))
                pps = lambda n, s: ph.enter_context(nc.psum_tensor(n, s, F32))
                qp = [psb(f"p4_qp{i}", [128, T], BF16) for i in (0, 1)]
                kp = [psb(f"p4_kp{i}", [128, T], BF16) for i in (0, 1)]
                vtp = [psb(f"p4_vtp{i}", [128, T], BF16) for i in (0, 1)]
                vpad = [psb(f"p4_vpad{i}", [128, 32 * 130], BF16) for i in (0, 1)]
                pt2 = [psb(f"p4_pt2{i}", [128, 1024], BF16) for i in (0, 1, 2)]
                ao = psb("p4_ao", [128, T], BF16)
                recA = psb("p4_recA", [1, T], F32)
                recB = psb("p4_recB", [1, T], F32)
                rsA = psb("p4_rsA", [1, T], F32)
                rsB = psb("p4_rsB", [1, T], F32)
                nb = psb("p4_nb", [128, 512], BF16)

                psS2 = [pps(f"p4_psS2{i}", [128, 1024]) for i in (0, 1, 2)]
                psPA = ph.enter_context(nc.psum_tensor("p4_psPA", [65, 512], F32))
                psPB = ph.enter_context(nc.psum_tensor("p4_psPB", [65, 512], F32))

                def load_pair(hp):
                    hb = hp % 2
                    prog.add("sync", lambda e, hp=hp, hb=hb: e.dma_start(
                        out=qp[hb][:], in_=qrot[hp * 128:(hp + 1) * 128, :]),
                        reads=[B_("qkstage")], writes=[B_(f"p4_qp{hb}")])
                    prog.add("sync", lambda e, hp=hp, hb=hb: e.dma_start(
                        out=kp[hb][:], in_=krot[hp * 128:(hp + 1) * 128, :]),
                        reads=[B_("qkstage")], writes=[B_(f"p4_kp{hb}")])
                    for kbr in range(4):
                        prog.add("sync", lambda e, hp=hp, hb=hb, kbr=kbr:
                                 e.dma_start(
                                     out=vtp[hb][:, kbr * 1024:(kbr + 1) * 1024]
                                     .rearrange("p (kb j) -> p kb j", j=128),
                                     in_=vt[kbr * 1024:(kbr + 1) * 1024,
                                            hp * 128:(hp + 1) * 128]
                                     .rearrange("(kb p) j -> p kb j", p=128)),
                                 reads=[B_("vtstage")], writes=[B_(f"p4_vtp{hb}")])
                    prog.add("dve", lambda e, hb=hb: e.memset(vpad[hb][:], 1.0),
                             writes=[B_(f"p4_vpad{hb}")])
                    for j0 in (0, 64):
                        prog.add("dve", lambda e, j0=j0, hb=hb: e.tensor_copy(
                            vpad[hb][:].rearrange("p (kb c) -> p kb c", c=130)[
                                :, :, j0 + (j0 // 64):j0 + (j0 // 64) + 64],
                            vtp[hb][:].rearrange("p (kb j) -> p kb j", j=128)[
                                :, :, j0:j0 + 64]),
                            reads=[B_(f"p4_vtp{hb}")], writes=[B_(f"p4_vpad{hb}")])

                load_pair(0)
                for hp in range(8):
                    hb = hp % 2
                    if hp + 1 < 8:
                        load_pair(hp + 1)
                    for qc in range(8):
                        nkt = (qc + 1) * 4

                        def emit_scores(ktb, qc=qc, hb=hb):
                            kp_ = ktb % 3
                            prog.add("pe", lambda e, ktb=ktb, qc=qc, kp_=kp_,
                                     hb=hb: e.matmul(
                                         psS2[kp_][:, 0:512],
                                         kp[hb][0:64, ktb * 128:(ktb + 1) * 128],
                                         qp[hb][0:64, qc * 512:(qc + 1) * 512],
                                         start=True, stop=True,
                                         tile_position=(0, 0)),
                                     reads=[B_(f"p4_kp{hb}"), B_(f"p4_qp{hb}")],
                                     writes=[B_(f"p4_psS2{kp_}")])
                            prog.add("pe", lambda e, ktb=ktb, qc=qc, kp_=kp_,
                                     hb=hb: e.matmul(
                                         psS2[kp_][:, 512:1024],
                                         kp[hb][64:128, ktb * 128:(ktb + 1) * 128],
                                         qp[hb][64:128, qc * 512:(qc + 1) * 512],
                                         start=True, stop=True,
                                         tile_position=(64, 0)),
                                     reads=[B_(f"p4_kp{hb}"), B_(f"p4_qp{hb}")],
                                     writes=[B_(f"p4_psS2{kp_}")])
                            prog.add("act", lambda e, kp_=kp_: e.activation(
                                pt2[kp_][:], psS2[kp_][:], AF.Exp),
                                reads=[B_(f"p4_psS2{kp_}")],
                                writes=[B_(f"p4_pt2{kp_}")])
                            if ktb >= qc * 4:
                                o = ktb - qc * 4
                                for hh in range(2):
                                    prog.add("dve", lambda e, o=o, kp_=kp_, hh=hh:
                                             e.tensor_mul(
                                                 pt2[kp_][:, hh * 512:
                                                          (hh + 1) * 512],
                                                 pt2[kp_][:, hh * 512:
                                                          (hh + 1) * 512],
                                                 mask_s[:, o * 512:(o + 1) * 512]),
                                             reads=[B_("mask_s")],
                                             writes=[B_(f"p4_pt2{kp_}")])

                        def emit_pv(ktb, qc=qc, nkt=nkt, hb=hb):
                            kp_ = ktb % 3
                            prog.add("pe", lambda e, ktb=ktb, nkt=nkt, kp_=kp_,
                                     hb=hb: e.matmul(
                                         psPA[:],
                                         vpad[hb][:, ktb * 130:ktb * 130 + 65],
                                         pt2[kp_][:, 0:512], start=(ktb == 0),
                                         stop=(ktb == nkt - 1)),
                                     reads=[B_(f"p4_vpad{hb}"),
                                            B_(f"p4_pt2{kp_}")],
                                     writes=[B_("p4_psPA")])
                            prog.add("pe", lambda e, ktb=ktb, nkt=nkt, kp_=kp_,
                                     hb=hb: e.matmul(
                                         psPB[:],
                                         vpad[hb][:, ktb * 130 + 65:
                                                  ktb * 130 + 130],
                                         pt2[kp_][:, 512:1024], start=(ktb == 0),
                                         stop=(ktb == nkt - 1)),
                                     reads=[B_(f"p4_vpad{hb}"),
                                            B_(f"p4_pt2{kp_}")],
                                     writes=[B_("p4_psPB")])

                        emit_scores(0)
                        if nkt > 1:
                            emit_scores(1)
                        for ktb in range(2, nkt):
                            emit_scores(ktb)
                            emit_pv(ktb - 2)
                        emit_pv(nkt - 2)
                        emit_pv(nkt - 1)
                        prog.add("act", lambda e, qc=qc: e.activation(
                            ao[0:64, qc * 512:(qc + 1) * 512], psPA[0:64, :],
                            AF.Copy),
                            reads=[B_("p4_psPA")], writes=[B_("p4_ao")])
                        prog.add("act", lambda e, qc=qc: e.activation(
                            ao[64:128, qc * 512:(qc + 1) * 512], psPB[0:64, :],
                            AF.Copy),
                            reads=[B_("p4_psPB")], writes=[B_("p4_ao")])
                        prog.add("act", lambda e, qc=qc: e.activation(
                            rsA[0:1, qc * 512:(qc + 1) * 512], psPA[64:65, :],
                            AF.Copy),
                            reads=[B_("p4_psPA")], writes=[B_("p4_rsA")])
                        prog.add("act", lambda e, qc=qc: e.activation(
                            rsB[0:1, qc * 512:(qc + 1) * 512], psPB[64:65, :],
                            AF.Copy),
                            reads=[B_("p4_psPB")], writes=[B_("p4_rsB")])
                        prog.add("dve", lambda e, qc=qc: e.reciprocal(
                            recA[0:1, qc * 512:(qc + 1) * 512],
                            rsA[0:1, qc * 512:(qc + 1) * 512]),
                            reads=[B_("p4_rsA")], writes=[B_("p4_recA")])
                        prog.add("dve", lambda e, qc=qc: e.reciprocal(
                            recB[0:1, qc * 512:(qc + 1) * 512],
                            rsB[0:1, qc * 512:(qc + 1) * 512]),
                            reads=[B_("p4_rsB")], writes=[B_("p4_recB")])
                    for qc in range(8):
                        for (rec, rnm, lo) in ((recA, "p4_recA", 0),
                                               (recB, "p4_recB", 64)):
                            prog.add("pe", lambda e, r=rec, qc=qc: e.matmul(
                                psPA[0:64, :], ones1[0:1, 0:64],
                                r[0:1, qc * 512:(qc + 1) * 512],
                                start=True, stop=True),
                                reads=[B_(rnm), B_("ones1")], writes=[B_("p4_psPA")])
                            prog.add("dve", lambda e, lo=lo: e.tensor_copy(
                                nb[lo:lo + 64, :], psPA[0:64, :]),
                                reads=[B_("p4_psPA")], writes=[B_("p4_nb")])
                            prog.add("dve", lambda e, lo=lo, qc=qc: e.tensor_mul(
                                ao[lo:lo + 64, qc * 512:(qc + 1) * 512],
                                ao[lo:lo + 64, qc * 512:(qc + 1) * 512],
                                nb[lo:lo + 64, :]),
                                reads=[B_("p4_nb")], writes=[B_("p4_ao")])
                    prog.add("sync", lambda e, hp=hp: e.dma_start(
                        out=attn_my[hp * 128:(hp + 1) * 128, :], in_=ao[:]),
                        reads=[B_("p4_ao")], writes=[B_("attn_my")])
                    drain_build(330)
                drain_build(10 ** 9)
                prog.add("pool", lambda e: e.collective_compute(
                    "AllGather", ALU.bypass, [[0, 1], [2, 3], [4, 5], [6, 7]],
                    ins=[attn_my.opt()], outs=[attn_g.opt()]),
                    reads=[B_("attn_my")], writes=[B_("attn_g")], inc=1)
                prog.add("pool", lambda e: e.collective_compute(
                    "AllGather", ALU.bypass, [[0, 1, 2, 3, 4, 5, 6, 7]],
                    ins=[wsh.opt()], outs=[wg8.opt()]),
                    reads=[B_("wsh")], writes=[B_("wg8")], inc=1)
                prog.barrier()
                _emit_block(nc, prog, sems)

        def ln_inline(tag, xo_t, xo_bufs, psA, psB, bA, bB, rows, dst_ap, tc,
                      psR, psM, bR, bM):
            prog.add("dve", lambda e: e.tensor_scalar_mul(
                rows["m"][0][:], psA[:], 1.0 / D),
                reads=[bA], writes=[rows["m"][1]])
            prog.add("dve", lambda e: e.tensor_scalar_mul(
                rows["q"][0][:], psB[:], 1.0 / D),
                reads=[bB], writes=[rows["q"][1]])
            prog.add("dve", lambda e: e.scalar_tensor_tensor(
                rows["r"][0][:], rows["m"][0][:], 1.0, rows["m"][0][:],
                ALU.mult, ALU.mult),
                reads=[rows["m"][1]], writes=[rows["r"][1]])
            prog.add("dve", lambda e: e.tensor_sub(
                rows["q"][0][:], rows["q"][0][:], rows["r"][0][:]),
                reads=[rows["q"][1], rows["r"][1]], writes=[rows["q"][1]])
            prog.add("act", lambda e: e.activation(
                rows["r"][0][:], rows["q"][0][:], AF.Ln, bias=cv[0:1, 0:1]),
                reads=[rows["q"][1], B_("c_cv")], writes=[rows["r"][1]])
            prog.add("act", lambda e: e.activation(
                rows["r"][0][:], rows["r"][0][:], AF.Exp, scale=cv[0:1, 1:2]),
                reads=[rows["r"][1]], writes=[rows["r"][1]])
            prog.add("dve", lambda e: e.scalar_tensor_tensor(
                rows["mb"][0][:], rows["m"][0][:], -1.0, rows["r"][0][:],
                ALU.mult, ALU.mult),
                reads=[rows["m"][1], rows["r"][1]], writes=[rows["mb"][1]])
            prog.add("pe", lambda e: e.matmul(
                psR[:], ones1[:], rows["r"][0][:], start=True, stop=True),
                reads=[rows["r"][1], B_("ones1")], writes=[bR])
            prog.add("pe", lambda e: e.matmul(
                psM[:], ones1[:], rows["mb"][0][:], start=True, stop=True),
                reads=[rows["mb"][1], B_("ones1")], writes=[bM])
            for ct in range(NCT):
                prog.add("dve", lambda e, ct=ct: e.tensor_mul(
                    rows["t1"][0][:], xo_t[ct][:], psR[:]),
                    reads=[xo_bufs[ct], bR], writes=[rows["t1"][1]])
                prog.add("dve", lambda e, ct=ct: e.tensor_add(
                    rows["xn"][0][:], rows["t1"][0][:], psM[:]),
                    reads=[rows["t1"][1], bM], writes=[rows["xn"][1]])
                prog.add("sync", lambda e, ct=ct, tc=tc: e.dma_start(
                    out=dst_ap[ct * 128:(ct + 1) * 128, tc * 512:(tc + 1) * 512],
                    in_=rows["xn"][0][:]),
                    reads=[rows["xn"][1]], writes=[B_(tag + "_dst")])

        if upto >= 4:
            with ExitStack() as ph:
                psb = lambda n, s, d: ph.enter_context(nc.sbuf_tensor(n, s, d))
                pps = lambda n, s: ph.enter_context(nc.psum_tensor(n, s, F32))
                wo_s = psb("p5_wo", [128, 16 * 2048], BF16)
                ac_s = psb("p5_ac", [128, 16 * 512], BF16)
                ah0 = psb("p5_ah0", [128, 512], BF16)
                ah1 = psb("p5_ah1", [128, 512], BF16)
                aht = psb("p5_aht", [128, 512], BF16)
                x0t = psb("p5_x0t", [128, 512], F32)
                xo_t = [psb(f"p5_xo{i}", [128, 512], F32) for i in range(NCT)]
                sq = psb("p5_sq", [128, 512], F32)
                rows = {k: (psb(f"p5_r{k}", [1, 512], F32), B_(f"p5_r{k}"))
                        for k in ("m", "q", "r", "mb")}
                rows["t1"] = (psb("p5_t1", [128, 512], F32), B_("p5_t1"))
                rows["xn"] = (psb("p5_xn", [128, 512], BF16), B_("p5_xn"))
                pso = [pps(f"p5_pso{i}", [128, 512]) for i in (0, 1)]
                psA = pps("p5_psA", [1, 512])
                psB = pps("p5_psB", [1, 512])
                psR = pps("p5_psR", [128, 512])
                psM = pps("p5_psM", [128, 512])
                prog.add("sync", lambda e: e.dma_start(
                    out=wo_s[:].rearrange("p (ct o) -> p ct o", o=2048),
                    in_=wo_bf.rearrange("(ct p) o -> p ct o", p=128)),
                    reads=[B_("wobf")], writes=[B_("p5_wo")])
                for tc in range(4):
                    for ct in range(NCT):
                        for half, dstb in ((0, ah0), (1, ah1)):
                            prog.add("sync", lambda e, ct=ct, tc=tc, half=half,
                                     d=dstb: e.dma_start(
                                         out=d[:],
                                         in_=attn_g[ct * 128:(ct + 1) * 128,
                                                    half * 2048 + tc * 512:
                                                    half * 2048 + (tc + 1) * 512]),
                                     reads=[B_("attn_g")],
                                     writes=[B_("p5_ah%d" % half)])
                        prog.add("dve", lambda e: e.tensor_scalar(
                            aht[:], ah1[:], sel1c[:, 0:1], None, ALU.mult),
                            reads=[B_("p5_ah1"), B_("c_sel")], writes=[B_("p5_aht")])
                        prog.add("dve", lambda e, ct=ct: e.scalar_tensor_tensor(
                            ac_s[:, ct * 512:(ct + 1) * 512], ah0[:],
                            sel0c[:, 0:1], aht[:], ALU.mult, ALU.add),
                            reads=[B_("p5_ah0"), B_("p5_aht"), B_("c_sel")],
                            writes=[B_("p5_ac")])
                    for ot in range(NCT):
                        op_ = ot % 2
                        for ct in range(NCT):
                            prog.add("pe", lambda e, ot=ot, ct=ct, op_=op_: e.matmul(
                                pso[op_][:],
                                wo_s[:, ct * 2048 + ot * 128:
                                     ct * 2048 + (ot + 1) * 128],
                                ac_s[:, ct * 512:(ct + 1) * 512],
                                start=(ct == 0), stop=(ct == NCT - 1)),
                                reads=[B_("p5_wo"), B_("p5_ac")],
                                writes=[B_(f"p5_pso{op_}")])
                        prog.add("act", lambda e, ot=ot, op_=op_: e.activation(
                            xo_t[ot][:], pso[op_][:], AF.Identity,
                            bias=bo_s[:, ot:ot + 1]),
                            reads=[B_(f"p5_pso{op_}"), B_("c_bo_s")],
                            writes=[B_(f"p5_xo{ot}")])
                        prog.add("sync", lambda e, ot=ot, tc=tc: e.dma_start(
                            out=x0t[:], in_=x0[ot * 128:(ot + 1) * 128,
                                              tc * 512:(tc + 1) * 512]),
                            writes=[B_("p5_x0t")])
                        prog.add("dve", lambda e, ot=ot: e.tensor_add(
                            xo_t[ot][:], xo_t[ot][:], x0t[:]),
                            reads=[B_("p5_x0t")], writes=[B_(f"p5_xo{ot}")])
                        prog.add("sync", lambda e, ot=ot, tc=tc: e.dma_start(
                            out=x1[ot * 128:(ot + 1) * 128,
                                   tc * 512:(tc + 1) * 512],
                            in_=xo_t[ot][:]),
                            reads=[B_(f"p5_xo{ot}")], writes=[B_("x1")])
                        prog.add("pe", lambda e, ot=ot: e.matmul(
                            psA[:], onesC[:], xo_t[ot][:], start=(ot == 0),
                            stop=(ot == NCT - 1)),
                            reads=[B_(f"p5_xo{ot}"), B_("onesC")],
                            writes=[B_("p5_psA")])
                        prog.add("act", lambda e, ot=ot: e.activation(
                            sq[:], xo_t[ot][:], AF.Square),
                            reads=[B_(f"p5_xo{ot}")], writes=[B_("p5_sq")])
                        prog.add("pe", lambda e, ot=ot: e.matmul(
                            psB[:], onesC[:], sq[:], start=(ot == 0),
                            stop=(ot == NCT - 1)),
                            reads=[B_("p5_sq"), B_("onesC")], writes=[B_("p5_psB")])
                    ln_inline("ln2", xo_t, [B_(f"p5_xo{i}") for i in range(NCT)],
                              psA, psB, B_("p5_psA"), B_("p5_psB"), rows, xn2, tc,
                              psR, psM, B_("p5_psR"), B_("p5_psM"))
                prog.barrier()
                _emit_block(nc, prog, sems)

        def ffn(tag, xn_ap, xres_ap, xo_ap, xn_next_ap):
            GOFF = 0
            UOFF = D * 768
            DOFF = 2 * D * 768
            with ExitStack() as ph:
                psb = lambda n, s, d: ph.enter_context(nc.sbuf_tensor(n, s, d))
                xn_s = psb(f"{tag}_xn", [128, 16 * 2048], BF16)
                wg_sb = [psb(f"{tag}_wg{i}", [128, 16 * 768], BF16) for i in (0, 1)]
                wu_sb = [psb(f"{tag}_wu{i}", [128, 16 * 768], BF16) for i in (0, 1)]
                gs = [psb(f"{tag}_gs{i}", [128, 512], BF16) for i in (0, 1)]
                us = [psb(f"{tag}_us{i}", [128, 512], BF16) for i in (0, 1)]
                ht = [psb(f"{tag}_ht{i}", [128, 512], BF16) for i in (0, 1)]
                with ExitStack() as phh:
                    ppsg = [phh.enter_context(
                        nc.psum_tensor(f"{tag}_psg{i}", [128, 512], F32))
                        for i in (0, 1)]
                    ppsu = [phh.enter_context(
                        nc.psum_tensor(f"{tag}_psu{i}", [128, 512], F32))
                        for i in (0, 1)]
                    for ct in range(NCT):
                        prog.add("sync", lambda e, ct=ct: e.dma_start(
                            out=xn_s[:, ct * 2048:(ct + 1) * 2048],
                            in_=xn_ap[ct * 128:(ct + 1) * 128, :]),
                            reads=[B_(tag + "_xnsrc")], writes=[B_(f"{tag}_xn")])
                    for r in range(8):
                        pb = r % 2
                        prog.add("sync", lambda e, r=r, pb=pb: e.dma_start(
                            out=wg_sb[pb][:],
                            in_=wg8[r, GOFF:GOFF + D * 768].rearrange(
                                "(p ci) -> p ci", p=128)),
                            reads=[B_("wg8")], writes=[B_(f"{tag}_wg{pb}")])
                        prog.add("sync", lambda e, r=r, pb=pb: e.dma_start(
                            out=wu_sb[pb][:],
                            in_=wg8[r, UOFF:UOFF + D * 768].rearrange(
                                "(p ci) -> p ci", p=128)),
                            reads=[B_("wg8")], writes=[B_(f"{tag}_wu{pb}")])
                        for il in range(6):
                            ig = r * 6 + il
                            if ig >= NIT:
                                continue
                            for sub in range(4):
                                gp = sub % 2
                                for ct in range(NCT):
                                    prog.add("pe", lambda e, pb=pb, ct=ct, il=il,
                                             sub=sub, gp=gp: e.matmul(
                                                 ppsg[gp][:],
                                                 wg_sb[pb][:, ct * 768 + il * 128:
                                                           ct * 768 + (il + 1) * 128],
                                                 xn_s[:, ct * 2048 + sub * 512:
                                                      ct * 2048 + (sub + 1) * 512],
                                                 start=(ct == 0),
                                                 stop=(ct == NCT - 1)),
                                             reads=[B_(f"{tag}_wg{pb}"),
                                                    B_(f"{tag}_xn")],
                                             writes=[B_(f"{tag}_psg{gp}")])
                                prog.add("act", lambda e, gp=gp: e.activation(
                                    gs[gp][:], ppsg[gp][:], AF.Silu),
                                    reads=[B_(f"{tag}_psg{gp}")],
                                    writes=[B_(f"{tag}_gs{gp}")])
                                for ct in range(NCT):
                                    prog.add("pe", lambda e, pb=pb, ct=ct, il=il,
                                             sub=sub, gp=gp: e.matmul(
                                                 ppsu[gp][:],
                                                 wu_sb[pb][:, ct * 768 + il * 128:
                                                           ct * 768 + (il + 1) * 128],
                                                 xn_s[:, ct * 2048 + sub * 512:
                                                      ct * 2048 + (sub + 1) * 512],
                                                 start=(ct == 0),
                                                 stop=(ct == NCT - 1)),
                                             reads=[B_(f"{tag}_wu{pb}"),
                                                    B_(f"{tag}_xn")],
                                             writes=[B_(f"{tag}_psu{gp}")])
                                prog.add("act", lambda e, gp=gp: e.activation(
                                    us[gp][:], ppsu[gp][:], AF.Copy),
                                    reads=[B_(f"{tag}_psu{gp}")],
                                    writes=[B_(f"{tag}_us{gp}")])
                                prog.add("dve", lambda e, gp=gp: e.tensor_mul(
                                    ht[gp][:], gs[gp][:], us[gp][:]),
                                    reads=[B_(f"{tag}_gs{gp}"), B_(f"{tag}_us{gp}")],
                                    writes=[B_(f"{tag}_ht{gp}")])
                                prog.add("sync", lambda e, ig=ig, sub=sub, gp=gp:
                                         e.dma_start(
                                             out=hbuf[ig * 128:(ig + 1) * 128,
                                                      sub * 512:(sub + 1) * 512],
                                             in_=ht[gp][:]),
                                         reads=[B_(f"{tag}_ht{gp}")],
                                         writes=[B_("hbuf")])
                prog.barrier()
                _emit_block(nc, prog, sems)
            final = xn_next_ap is None
            with ExitStack() as ph:
                psb = lambda n, s, d: ph.enter_context(nc.sbuf_tensor(n, s, d))
                pps = lambda n, s: ph.enter_context(nc.psum_tensor(n, s, F32))
                h_s = psb(f"{tag}_hs", [128, NIT * 1024], BF16)
                wd_sb = [psb(f"{tag}_wd{i}", [128, NIT * 256], BF16) for i in (0, 1)]
                xrt = [psb(f"{tag}_xrt{i}", [128, 1024], F32) for i in (0, 1)]
                xod = [psb(f"{tag}_xod{i}", [128, 1024], F32) for i in (0, 1)]
                psd2 = [pps(f"{tag}_psd2{i}", [128, 1024]) for i in (0, 1)]
                for sp in range(2):
                    for part in range(4):
                        prog.add("sync", lambda e, sp=sp, part=part: e.dma_start(
                            out=h_s[:].rearrange("p (it t) -> p it t", t=1024)[
                                :, part * 11:min((part + 1) * 11, NIT), :],
                            in_=hbuf[:, sp * 1024:(sp + 1) * 1024].rearrange(
                                "(it p) t -> p it t", p=128)[
                                :, part * 11:min((part + 1) * 11, NIT), :]),
                            reads=[B_("hbuf")], writes=[B_(f"{tag}_hs")])
                    for r in range(8):
                        pb = r % 2
                        prog.add("sync", lambda e, r=r, pb=pb: e.dma_start(
                            out=wd_sb[pb][:],
                            in_=wg8[r, DOFF:DOFF + INTER * 256].rearrange(
                                "(p io) -> p io", p=128)),
                            reads=[B_("wg8")], writes=[B_(f"{tag}_wd{pb}")])
                        for otl in range(2):
                            ot = r * 2 + otl
                            dp = (r * 2 + otl) % 2
                            for half in range(2):
                                for it in range(NIT):
                                    prog.add("pe", lambda e, pb=pb, it=it, otl=otl,
                                             dp=dp, half=half: e.matmul(
                                                 psd2[dp][:, half * 512:
                                                          (half + 1) * 512],
                                                 wd_sb[pb][:, it * 256 + otl * 128:
                                                           it * 256 +
                                                           (otl + 1) * 128],
                                                 h_s[:, it * 1024 + half * 512:
                                                     it * 1024 + (half + 1) * 512],
                                                 start=(it == 0),
                                                 stop=(it == NIT - 1)),
                                             reads=[B_(f"{tag}_wd{pb}"),
                                                    B_(f"{tag}_hs")],
                                             writes=[B_(f"{tag}_psd2{dp}")])
                            prog.add("sync", lambda e, ot=ot, sp=sp, dp=dp:
                                     e.dma_start(
                                         out=xrt[dp][:],
                                         in_=xres_ap[ot * 128:(ot + 1) * 128,
                                                     sp * 1024:(sp + 1) * 1024]),
                                     reads=[B_(tag + "_xressrc")],
                                     writes=[B_(f"{tag}_xrt{dp}")])
                            prog.add("dve", lambda e, dp=dp: e.tensor_add(
                                xod[dp][:], psd2[dp][:], xrt[dp][:]),
                                reads=[B_(f"{tag}_psd2{dp}"), B_(f"{tag}_xrt{dp}")],
                                writes=[B_(f"{tag}_xod{dp}")])
                            prog.add("sync", lambda e, ot=ot, sp=sp, dp=dp:
                                     e.dma_start(
                                         out=xo_ap[ot * 128:(ot + 1) * 128,
                                                   sp * 1024:(sp + 1) * 1024],
                                         in_=xod[dp][:]),
                                     reads=[B_(f"{tag}_xod{dp}")],
                                     writes=[B_(tag + "_xo")])
                prog.barrier()
                _emit_block(nc, prog, sems)
            if not final:
                ln_pass(tag + "ln", xo_ap, xn_next_ap)

        if upto >= 5:
            bufs["f1_xnsrc"] = bufs.get("ln2_dst", Buf("f1_xnsrc"))
            bufs["f1_xressrc"] = bufs.get("x1", Buf("f1_xressrc"))
            ffn("f1", xn2, x1, x2, xn3)
            bufs["x2"] = bufs["f1_xo"]
            bufs["xn3_dst"] = bufs["f1ln_dst"]

        if upto >= 6:
          with ExitStack() as p7st:
            with ExitStack() as ph:
                psb = lambda n, s, d: ph.enter_context(nc.sbuf_tensor(n, s, d))
                pps = lambda n, s: ph.enter_context(nc.psum_tensor(n, s, F32))
                pin_sb = psb("p7_pin", [128, 16 * 2048], BF16)
                xn_c = psb("p7_xnc", [128, 16 * 512], BF16)
                xpf = psb("p7_xpf", [128, 512], F32)
                xpb = psb("p7_xpb", [128, 512], BF16)
                psxp = [pps(f"p7_psxp{i}", [128, 512]) for i in (0, 1)]
                prog.add("sync", lambda e: e.dma_start(
                    out=pin_sb[:].rearrange("p (ct o) -> p ct o", o=2048),
                    in_=pin_bf.rearrange("(ct p) o -> p ct o", p=128)),
                    reads=[B_("pinbf")], writes=[B_("p7_pin")])
                for tc in range(4):
                    for ct in range(NCT):
                        prog.add("sync", lambda e, ct=ct, tc=tc: e.dma_start(
                            out=xn_c[:, ct * 512:(ct + 1) * 512],
                            in_=xn3[ct * 128:(ct + 1) * 128,
                                    tc * 512:(tc + 1) * 512]),
                            reads=[B_("xn3_dst")], writes=[B_("p7_xnc")])
                    for dt in range(NCT):
                        xp_ = dt % 2
                        for ct in range(NCT):
                            prog.add("pe", lambda e, dt=dt, ct=ct, xp_=xp_: e.matmul(
                                psxp[xp_][:],
                                pin_sb[:, ct * 2048 + dt * 128:
                                       ct * 2048 + (dt + 1) * 128],
                                xn_c[:, ct * 512:(ct + 1) * 512],
                                start=(ct == 0), stop=(ct == NCT - 1)),
                                reads=[B_("p7_pin"), B_("p7_xnc")],
                                writes=[B_(f"p7_psxp{xp_}")])
                        prog.add("act", lambda e, dt=dt, xp_=xp_: e.activation(
                            xpf[:], psxp[xp_][:], AF.Identity,
                            bias=pinb_s[:, dt:dt + 1]),
                            reads=[B_(f"p7_psxp{xp_}"), B_("c_pinb_s")],
                            writes=[B_("p7_xpf")])
                        prog.add("dve", lambda e: e.tensor_copy(xpb[:], xpf[:]),
                            reads=[B_("p7_xpf")], writes=[B_("p7_xpb")])
                        prog.add("sync", lambda e, dt=dt, tc=tc: e.dma_start(
                            out=xp_my[dt * 128:(dt + 1) * 128,
                                      tc * 512:(tc + 1) * 512],
                            in_=xpb[:]),
                            reads=[B_("p7_xpb")], writes=[B_("xp_my")])
                prog.add("pool", lambda e: e.collective_compute(
                    "AllGather", ALU.bypass, [[0, 1], [2, 3], [4, 5], [6, 7]],
                    ins=[xp_my.opt()], outs=[xp_g.opt()]),
                    reads=[B_("xp_my")], writes=[B_("xp_g")], inc=1)
                xpdt = psb("p7_xpdt", [128, 512], BF16)
                dl0 = psb("p7_dl0", [16, 512], F32)
                dl1 = psb("p7_dl1", [16, 512], F32)
                dT_s = psb("p7_dTs", [16, 2048], BF16)
                psdl = pps("p7_psdl", [16, 512])
                import os as _os2
                _lvl = int(_os2.environ.get("P7B_LVL", "6"))
                for tch in range(4):
                    for dt in range(NCT):
                        prog.add("sync", lambda e, dt=dt, tch=tch: e.dma_start(
                            out=xpdt[:], in_=xp_my[dt * 128:(dt + 1) * 128,
                                                   tch * 512:(tch + 1) * 512]),
                            reads=[B_("xp_my")], writes=[B_("p7_xpdt")])
                        if _lvl < 2:
                            continue
                        prog.add("pe", lambda e, dt=dt: e.matmul(
                            psdl[:], BwT_s[:, dt * 16:(dt + 1) * 16], xpdt[:],
                            start=(dt == 0), stop=(dt == NCT - 1)),
                            reads=[B_("BwT_s"), B_("p7_xpdt")],
                            writes=[B_("p7_psdl")])
                    if _lvl < 3:
                        continue
                    prog.add("act", lambda e: e.activation(dl0[:], psdl[:], AF.Abs),
                             reads=[B_("p7_psdl")], writes=[B_("p7_dl0")])
                    prog.add("act", lambda e: e.activation(
                        dl0[:], dl0[:], AF.Exp, scale=-1.0),
                        reads=[B_("p7_dl0")], writes=[B_("p7_dl0")])
                    prog.add("act", lambda e: e.activation(
                        dl0[:], dl0[:], AF.Ln, bias=1.0),
                        reads=[B_("p7_dl0")], writes=[B_("p7_dl0")])
                    if _lvl < 4:
                        continue
                    prog.add("dve", lambda e: e.tensor_scalar_max(
                        dl1[:], psdl[:], 0.0),
                        reads=[B_("p7_psdl")], writes=[B_("p7_dl1")])
                    if _lvl < 5:
                        continue
                    prog.add("dve", lambda e, tch=tch: e.tensor_add(
                        dT_s[:, tch * 512:(tch + 1) * 512], dl0[:], dl1[:]),
                        reads=[B_("p7_dl0"), B_("p7_dl1")], writes=[B_("p7_dTs")])
                if int(_os2.environ.get("P7B_LVL", "6")) >= 6:
                    prog.add("sync", lambda e: e.dma_start(out=dT_my, in_=dT_s[:]),
                             reads=[B_("p7_dTs")], writes=[B_("dT_my")])
                import os as _os
                if _os.environ.get("P7B_AG", "1") == "1":
                    prog.add("pool", lambda e: e.collective_compute(
                        "AllGather", ALU.bypass, [[0, 1], [2, 3], [4, 5], [6, 7]],
                        ins=[dT_my.opt()], outs=[dT_g.opt()]),
                        reads=[B_("dT_my")], writes=[B_("dT_g")], inc=1)
                prog.barrier()
                _emit_block(nc, prog, sems)
            if p7sub >= 3:
              with ExitStack() as ph:
                psb = lambda n, s, d: ph.enter_context(nc.sbuf_tensor(n, s, d))
                pps = lambda n, s: ph.enter_context(nc.psum_tensor(n, s, F32))
                dr0 = psb("p7_dr0", [1, 2048], BF16)
                dr1 = psb("p7_dr1", [1, 2048], BF16)
                drt = psb("p7_drt", [1, 2048], BF16)
                drc = psb("p7_drc", [1, 2048], BF16)
                dbc = psb("p7_dbc", [128, 8 * 2048], BF16)
                xpr = [psb(f"p7_xpr{i}", [128, 2048], BF16) for i in (0, 1)]
                ut = {en: psb(f"p7_ut_{en}", [128, 2048], BF16)
                      for en in ("dve", "pool")}
                htl = {en: psb(f"p7_htl_{en}", [128, 2048], BF16)
                       for en in ("dve", "pool")}
                carry = psb("p7_carry", [128, 128], F32)
                G_s = psb("p7_Gs", [8, T], BF16)
                psBC = pps("p7_psBC", [128, 512])
                psG = [pps(f"p7_psG{i}", [8, 512]) for i in range(4)]
                for th in range(2):
                    for s in range(8):
                        prog.add("sync", lambda e, s=s, th=th: e.dma_start(
                            out=dr0[:], in_=dT_g[th * 16 + s:th * 16 + s + 1, :]),
                            reads=[B_("dT_g")], writes=[B_("p7_dr0")])
                        prog.add("sync", lambda e, s=s, th=th: e.dma_start(
                            out=dr1[:], in_=dT_g[th * 16 + 8 + s:
                                                 th * 16 + 8 + s + 1, :]),
                            reads=[B_("dT_g")], writes=[B_("p7_dr1")])
                        prog.add("dve", lambda e: e.tensor_scalar(
                            drt[:], dr1[:], sel1c[0:1, 0:1], None, ALU.mult),
                            reads=[B_("p7_dr1"), B_("c_sel")], writes=[B_("p7_drt")])
                        prog.add("dve", lambda e: e.scalar_tensor_tensor(
                            drc[:], dr0[:], sel0c[0:1, 0:1], drt[:],
                            ALU.mult, ALU.add),
                            reads=[B_("p7_dr0"), B_("p7_drt"), B_("c_sel")],
                            writes=[B_("p7_drc")])
                        for ch in range(4):
                            prog.add("pe", lambda e, ch=ch: e.matmul(
                                psBC[:], ones1b[:],
                                drc[0:1, ch * 512:(ch + 1) * 512],
                                start=True, stop=True),
                                reads=[B_("p7_drc"), B_("ones1b")],
                                writes=[B_("p7_psBC")])
                            prog.add("dve", lambda e, s=s, ch=ch: e.tensor_copy(
                                dbc[:, s * 2048 + ch * 512:
                                    s * 2048 + (ch + 1) * 512], psBC[:]),
                                reads=[B_("p7_psBC")], writes=[B_("p7_dbc")])
                    for dt in range(NCT):
                        pb = dt % 2
                        prog.add("sync", lambda e, dt=dt, th=th, pb=pb:
                                 e.dma_start(
                                     out=xpr[pb][:],
                                     in_=xp_g[th * D + dt * 128:
                                              th * D + (dt + 1) * 128, :]),
                                 reads=[B_("xp_g")],
                                 writes=[B_(f"p7_xpr{pb}")])
                        for s in range(8):
                            en = "dve" if s % 2 == 0 else "pool"
                            prog.add(en, lambda e, s=s, pb=pb, en=en: e.tensor_mul(
                                ut[en][:], xpr[pb][:],
                                dbc[:, s * 2048:(s + 1) * 2048]),
                                reads=[B_(f"p7_xpr{pb}"), B_("p7_dbc")],
                                writes=[B_(f"p7_ut_{en}")])
                            col = dt * 8 + s
                            if th == 0:
                                prog.add("dve", lambda e, col=col, en=en:
                                         e.tensor_tensor_scan(
                                             htl[en][:],
                                             sigA[:, col:col + 1].broadcast_to(
                                                 [128, 2048]),
                                             ut[en][:], 0.0, ALU.mult, ALU.add),
                                         reads=[B_(f"p7_ut_{en}"), B_("sigA")],
                                         writes=[B_(f"p7_htl_{en}")])
                                prog.add("dve", lambda e, col=col, en=en:
                                         e.tensor_copy(
                                             carry[:, col:col + 1],
                                             htl[en][:, 2047:2048]),
                                         reads=[B_(f"p7_htl_{en}")],
                                         writes=[B_("p7_carry")])
                            else:
                                prog.add("dve", lambda e, col=col, en=en:
                                         e.tensor_tensor_scan(
                                             htl[en][:],
                                             sigA[:, col:col + 1].broadcast_to(
                                                 [128, 2048]),
                                             ut[en][:], carry[:, col:col + 1],
                                             ALU.mult, ALU.add),
                                         reads=[B_(f"p7_ut_{en}"), B_("sigA"),
                                                B_("p7_carry")],
                                         writes=[B_(f"p7_htl_{en}")])
                            for ch in range(4):
                                prog.add("pe", lambda e, s=s, ch=ch, dt=dt, en=en:
                                         e.matmul(
                                             psG[ch][:],
                                             E_s[:, s * 8:(s + 1) * 8],
                                             htl[en][:, ch * 512:(ch + 1) * 512],
                                             start=(s == 0 and dt == 0),
                                             stop=(s == 7 and dt == NCT - 1)),
                                         reads=[B_(f"p7_htl_{en}"), B_("E_s")],
                                         writes=[B_(f"p7_psG{ch}")])
                    for ch in range(4):
                        prog.add("dve", lambda e, th=th, ch=ch:
                                 e.tensor_copy(
                                     G_s[:, th * 2048 + ch * 512:
                                         th * 2048 + (ch + 1) * 512],
                                     psG[ch][:]),
                                 reads=[B_(f"p7_psG{ch}")], writes=[B_("p7_Gs")])
                prog.add("sync", lambda e: e.dma_start(out=G_my, in_=G_s[:]),
                         reads=[B_("p7_Gs")], writes=[B_("G_my")])
                prog.add("pool", lambda e: e.collective_compute(
                    "AllGather", ALU.bypass, [[0, 1], [2, 3], [4, 5], [6, 7]],
                    ins=[G_my.opt()], outs=[G_g.opt()]),
                    reads=[B_("G_my")], writes=[B_("G_g")], inc=1)
                prog.barrier()
                _emit_block(nc, prog, sems)
            if p7sub >= 4:
              y_s = p7st.enter_context(
                nc.sbuf_tensor("p7_ys", [128, 16 * 2048], BF16))
              with ExitStack() as ph:
                  psb = lambda n, s, d: ph.enter_context(nc.sbuf_tensor(n, s, d))
                  pps = lambda n, s: ph.enter_context(nc.psum_tensor(n, s, F32))
                  gl0 = psb("p7_gl0", [16, 512], BF16)
                  gl1 = psb("p7_gl1", [16, 512], BF16)
                  glt = psb("p7_glt", [16, 512], BF16)
                  G_loc = psb("p7_gloc", [16, 2048], BF16)
                  xpd = psb("p7_xpd", [128, 512], F32)
                  xpdt2 = psb("p7_xpdt2", [128, 2048], BF16)
                  psy = pps("p7_psy", [128, 512])
                  for ch in range(4):
                      for half, dstb in ((0, gl0), (1, gl1)):
                          prog.add("sync", lambda e, ch=ch, half=half, d=dstb:
                                   e.dma_start(
                                       out=d[:],
                                       in_=G_g[:, half * 2048 + ch * 512:
                                               half * 2048 + (ch + 1) * 512]),
                                   reads=[B_("G_g")], writes=[B_("p7_gl%d" % half)])
                      prog.add("dve", lambda e: e.tensor_scalar(
                          glt[:], gl1[:], sel1c[0:16, 0:1], None, ALU.mult),
                          reads=[B_("p7_gl1"), B_("c_sel")], writes=[B_("p7_glt")])
                      prog.add("dve", lambda e, ch=ch: e.scalar_tensor_tensor(
                          G_loc[:, ch * 512:(ch + 1) * 512], gl0[:],
                          sel0c[0:16, 0:1], glt[:], ALU.mult, ALU.add),
                          reads=[B_("p7_gl0"), B_("p7_glt"), B_("c_sel")],
                          writes=[B_("p7_gloc")])
                  for dt in range(NCT):
                      prog.add("sync", lambda e, dt=dt: e.dma_start(
                          out=xpdt2[:], in_=xp_my[dt * 128:(dt + 1) * 128, :]),
                          reads=[B_("xp_my")], writes=[B_("p7_xpdt2")])
                      for ch in range(4):
                          prog.add("pe", lambda e, dt=dt, ch=ch: e.matmul(
                              psy[:], CwT_s[:, dt * 128:(dt + 1) * 128],
                              G_loc[:, ch * 512:(ch + 1) * 512],
                              start=True, stop=True),
                              reads=[B_("CwT_s"), B_("p7_gloc")], writes=[B_("p7_psy")])
                          prog.add("dve", lambda e, dt=dt, ch=ch: e.tensor_scalar(
                              xpd[:], xpdt2[:, ch * 512:(ch + 1) * 512],
                              D_s[:, dt:dt + 1], None, ALU.mult),
                              reads=[B_("p7_xpdt2"), B_("c_D_s")], writes=[B_("p7_xpd")])
                          prog.add("dve", lambda e, dt=dt, ch=ch: e.tensor_add(
                              y_s[:, dt * 2048 + ch * 512:dt * 2048 + (ch + 1) * 512],
                              psy[:], xpd[:]),
                              reads=[B_("p7_psy"), B_("p7_xpd")], writes=[B_("p7_ys")])
                  prog.barrier()
                  _emit_block(nc, prog, sems)
              with ExitStack() as ph:
                  psb = lambda n, s, d: ph.enter_context(nc.sbuf_tensor(n, s, d))
                  pps = lambda n, s: ph.enter_context(nc.psum_tensor(n, s, F32))
                  pout_sb = psb("p7_pout", [128, 16 * 2048], BF16)
                  x2t = psb("p7_x2t", [128, 512], F32)
                  xn3t = psb("p7_xn3t", [128, 512], BF16)
                  xn3f = psb("p7_xn3f", [128, 512], F32)
                  xo_t = [psb(f"p7_xo{i}", [128, 512], F32) for i in range(NCT)]
                  sq = psb("p7_sq", [128, 512], F32)
                  rows = {k: (psb(f"p7_r{k}", [1, 512], F32), B_(f"p7_r{k}"))
                          for k in ("m", "q", "r", "mb")}
                  rows["t1"] = (psb("p7_t1", [128, 512], F32), B_("p7_t1"))
                  rows["xn"] = (psb("p7_xnn", [128, 512], BF16), B_("p7_xnn"))
                  psp = [pps(f"p7_psp{i}", [128, 512]) for i in (0, 1)]
                  psA = pps("p7_psA2", [1, 512])
                  psB = pps("p7_psB2", [1, 512])
                  psR = pps("p7_psR2", [128, 512])
                  psM = pps("p7_psM2", [128, 512])
                  prog.add("sync", lambda e: e.dma_start(
                      out=pout_sb[:].rearrange("p (ct o) -> p ct o", o=2048),
                      in_=pout_bf.rearrange("(ct p) o -> p ct o", p=128)),
                      reads=[B_("poutbf")], writes=[B_("p7_pout")])
                  if DBG7:
                      prog.add("sync", lambda e: e.dma_start(
                          out=dbg_pout, in_=pout_sb[:]),
                          reads=[B_("p7_pout")], writes=[B_("dbgp")])
                      prog.add("sync", lambda e: e.dma_start(
                          out=dbg_y, in_=y_s[:]),
                          reads=[B_("p7_ys")], writes=[B_("dbgy")])
                  for tc in range(4):
                      for ot in range(NCT):
                          op_ = ot % 2
                          for dt in range(NCT):
                              prog.add("pe", lambda e, ot=ot, dt=dt, tc=tc,
                                       op_=op_: e.matmul(
                                           psp[op_][:],
                                           pout_sb[:, dt * 2048 + ot * 128:
                                                   dt * 2048 + (ot + 1) * 128],
                                           y_s[:, dt * 2048 + tc * 512:
                                               dt * 2048 + (tc + 1) * 512],
                                           start=(dt == 0), stop=(dt == NCT - 1)),
                                       reads=[B_("p7_pout"), B_("p7_ys")],
                                       writes=[B_(f"p7_psp{op_}")])
                          prog.add("act", lambda e, ot=ot, op_=op_: e.activation(
                              xo_t[ot][:], psp[op_][:], AF.Identity,
                              bias=poutb_s[:, ot:ot + 1]),
                              reads=[B_(f"p7_psp{op_}"), B_("c_poutb_s")],
                              writes=[B_(f"p7_xo{ot}")])
                          prog.add("sync", lambda e, ot=ot, tc=tc: e.dma_start(
                              out=xn3t[:], in_=xn3[ot * 128:(ot + 1) * 128,
                                                 tc * 512:(tc + 1) * 512]),
                              reads=[B_("xn3_dst")], writes=[B_("p7_xn3t")])
                          prog.add("dve", lambda e: e.tensor_copy(xn3f[:], xn3t[:]),
                                   reads=[B_("p7_xn3t")], writes=[B_("p7_xn3f")])
                          prog.add("dve", lambda e, ot=ot: e.tensor_add(
                              xo_t[ot][:], xo_t[ot][:], xn3f[:]),
                              reads=[B_("p7_xn3f")], writes=[B_(f"p7_xo{ot}")])
                          prog.add("sync", lambda e, ot=ot, tc=tc: e.dma_start(
                              out=x2t[:], in_=x2[ot * 128:(ot + 1) * 128,
                                                tc * 512:(tc + 1) * 512]),
                              reads=[B_("x2")], writes=[B_("p7_x2t")])
                          prog.add("dve", lambda e, ot=ot: e.tensor_add(
                              xo_t[ot][:], xo_t[ot][:], x2t[:]),
                              reads=[B_("p7_x2t")], writes=[B_(f"p7_xo{ot}")])
                          prog.add("sync", lambda e, ot=ot, tc=tc: e.dma_start(
                              out=x3[ot * 128:(ot + 1) * 128,
                                     tc * 512:(tc + 1) * 512],
                              in_=xo_t[ot][:]),
                              reads=[B_(f"p7_xo{ot}")], writes=[B_("x3")])
                          prog.add("pe", lambda e, ot=ot: e.matmul(
                              psA[:], onesC[:], xo_t[ot][:], start=(ot == 0),
                              stop=(ot == NCT - 1)),
                              reads=[B_(f"p7_xo{ot}"), B_("onesC")],
                              writes=[B_("p7_psA2")])
                          prog.add("act", lambda e, ot=ot: e.activation(
                              sq[:], xo_t[ot][:], AF.Square),
                              reads=[B_(f"p7_xo{ot}")], writes=[B_("p7_sq")])
                          prog.add("pe", lambda e, ot=ot: e.matmul(
                              psB[:], onesC[:], sq[:], start=(ot == 0),
                              stop=(ot == NCT - 1)),
                              reads=[B_("p7_sq"), B_("onesC")], writes=[B_("p7_psB2")])
                      ln_inline("ln4", xo_t, [B_(f"p7_xo{i}") for i in range(NCT)],
                                psA, psB, B_("p7_psA2"), B_("p7_psB2"), rows, xn4,
                                tc, psR, psM, B_("p7_psR2"), B_("p7_psM2"))
                  prog.barrier()
                  _emit_block(nc, prog, sems)

        if upto >= 7:
            bufs["f2_xnsrc"] = bufs.get("ln4_dst", Buf("f2_xnsrc"))
            bufs["f2_xressrc"] = bufs.get("x3", Buf("f2_xressrc"))
            ffn("f2", xn4, x3, xout, None)
            bufs["xout"] = bufs["f2_xo"]

        with ExitStack() as ph:
            prog.barrier()
            emit_taps()
            if upto < 7:
                tcopy = ph.enter_context(nc.sbuf_tensor("stub_t", [128, 2048], F32))
                for ct in range(NCT):
                    prog.add("sync", lambda e, ct=ct: e.dma_start(
                        out=tcopy[:], in_=x0[ct * 128:(ct + 1) * 128, :]),
                        writes=[B_("stub")])
                    prog.add("sync", lambda e, ct=ct: e.dma_start(
                        out=xout[ct * 128:(ct + 1) * 128, :], in_=tcopy[:]),
                        reads=[B_("stub")], writes=[B_("xout")])
            prog.barrier()
            _emit_block(nc, prog, sems, final=True)
        return nc



def _rope_tables():
    i = np.arange(32, dtype=np.float64)
    theta = 10000.0 ** (-2.0 * i / 64.0)
    t = np.arange(T, dtype=np.float64)
    tid = (np.arange(T) // 576) // 2
    ang = (t + tid)[:, None] * theta[None, :]
    c = np.cos(ang).T
    s = np.sin(ang).T
    Cfull = np.empty((128, T), np.float64)
    Sfull = np.empty((128, T), np.float64)
    for blk in range(4):
        r = slice(blk * 32, blk * 32 + 32)
        Cfull[r] = c
        Sfull[r] = s if (blk % 2 == 1) else -s
    return Cfull, Sfull


def _host_inputs(full, c):
    b, p = c // 2, c % 2
    f32 = np.float32
    d = {}
    xs = full["L0_attn_a_hidden_states"][b, p * NT:(p + 1) * NT, :]
    d["x0"] = np.ascontiguousarray(xs.T.astype(f32))
    hsl = slice(p * 1024, (p + 1) * 1024)
    d["wqT"] = np.ascontiguousarray(full["L0_attn_a_Wq"].T[:, hsl].astype(f32))
    d["wkT"] = np.ascontiguousarray(full["L0_attn_a_Wk"].T[:, hsl].astype(f32))
    d["wvT"] = np.ascontiguousarray(full["L0_attn_a_Wv"].T[:, hsl].astype(f32))
    d["woT"] = np.ascontiguousarray(full["L0_attn_a_Wo"].T.astype(f32))
    d["bq"] = np.ascontiguousarray(
        full["L0_attn_a_bq"][hsl].reshape(8, 128).T.astype(f32))
    d["bk"] = np.ascontiguousarray(
        full["L0_attn_a_bk"][hsl].reshape(8, 128).T.astype(f32))
    d["bv"] = np.ascontiguousarray(
        full["L0_attn_a_bv"][hsl].reshape(1, 1024).astype(f32))
    d["bo"] = np.ascontiguousarray(
        full["L0_attn_a_bo"].reshape(16, 128).T.astype(f32))
    d["pinT"] = np.ascontiguousarray(full["L0_attn_b_proj_in_w"].T.astype(f32))
    d["pinb"] = np.ascontiguousarray(
        full["L0_attn_b_proj_in_b"].reshape(16, 128).T.astype(f32))
    d["poutT"] = np.ascontiguousarray(full["L0_attn_b_proj_out_w"].T.astype(f32))
    d["poutb"] = np.ascontiguousarray(
        full["L0_attn_b_proj_out_b"].reshape(16, 128).T.astype(f32))
    d["BwT"] = np.ascontiguousarray(full["L0_attn_b_B_w"].T.astype(f32))
    d["CwT"] = np.ascontiguousarray(full["L0_attn_b_C_w"].T.astype(f32))
    d["Amat"] = np.ascontiguousarray(full["L0_attn_b_A"][:, p * 8:(p + 1) * 8].astype(f32))
    d["Dpar"] = np.ascontiguousarray(
        full["L0_attn_b_D_param"].reshape(16, 128).T.astype(f32))
    d["lut"] = np.ascontiguousarray(
        np.tile(full["L0_ffn_lut"].reshape(1, 16), (128, 1)).astype(f32))
    gi = np.zeros((D, IPAD), np.int32)
    gi[:, :INTER] = full["L0_ffn_gate_idx"].T
    ui = np.zeros((D, IPAD), np.int32)
    ui[:, :INTER] = full["L0_ffn_up_idx"].T
    d["gidxT"] = np.ascontiguousarray(gi[:, c * 768:(c + 1) * 768])
    d["uidxT"] = np.ascontiguousarray(ui[:, c * 768:(c + 1) * 768])
    d["didxT"] = np.ascontiguousarray(
        full["L0_ffn_down_idx"].T[:, c * 256:(c + 1) * 256].astype(np.int32))
    Cf, Sf = _rope_tables()
    d["ropeCq"] = np.ascontiguousarray((0.125 * Cf).astype(f32))
    d["ropeSq"] = np.ascontiguousarray((0.125 * Sf).astype(f32))
    d["ropeCk"] = np.ascontiguousarray(Cf.astype(f32))
    d["ropeSk"] = np.ascontiguousarray(Sf.astype(f32))
    m = np.zeros((128, 2048), np.float32)
    for o in range(4):
        k = np.arange(128)[:, None]
        q = np.arange(512)[None, :]
        m[:, o * 512:(o + 1) * 512] = (o * 128 + k <= q)
    d["dmask"] = m.astype(BF)
    E = np.zeros((128, 64), np.float32)
    for s_ in range(8):
        E[:, s_ * 8 + s_] = 1.0
    d["Eall"] = E.astype(BF)
    d["identb"] = np.eye(128).astype(BF)
    d["sel0"] = np.full((128, 1), 1.0 - p, np.float32)
    d["sel1"] = np.full((128, 1), float(p), np.float32)
    cva = np.zeros((1, 8), np.float32)
    cva[0, 0] = EPS
    cva[0, 1] = -0.5
    cva[0, 2] = 1.0 / D
    d["cvals"] = np.ascontiguousarray(np.tile(cva, (128, 1)))
    return d


def kernel(**inputs):
    nc = build()
    in_maps = [_host_inputs(inputs, c) for c in range(8)]
    res = run_bass_kernel_spmd(nc, in_maps, core_ids=list(range(8)))
    out = np.empty((B, T, D), np.float32)
    for c in range(8):
        b, p = c // 2, c % 2
        out[b, p * NT:(p + 1) * NT, :] = np.asarray(
            res.results[c]["xout"], dtype=np.float32).T
    return out
